# version 02
# baseline (speedup 1.0000x reference)
import math
from contextlib import ExitStack

import numpy as np
import concourse.bass as bass
import concourse.mybir as mybir
from concourse.bass_utils import run_bass_kernel_spmd

F32 = mybir.dt.float32
BF16 = mybir.dt.bfloat16
I32 = mybir.dt.int32
AF = mybir.ActivationFunctionType
ALU = mybir.AluOpType
AX = mybir.AxisListType.X

D = 1024
HID = 4096
TOK_OWN = 2048
TOK_FULL = 4096
PNORM = 1.0 / math.sqrt(math.sqrt(64.0))
RG = [[0, 1], [2, 3], [4, 5], [6, 7]]

DMA_ENG = ("q", "g")
ENGS = ("t", "v", "s", "q", "g")

_TAPS = ()
_STAGE_MAX = 99
_EMIT_LIM = None


class Prog:
    def __init__(self):
        self.items = {e: [] for e in ENGS}
        self.cnt = {e: 0 for e in ENGS}
        self.cnt["c"] = 0
        self.seq = 0
        self.marks = []

    def mark(self, label):
        self.marks.append((label, self.seq))

    def op(self, eng, fn, after=(), inc=False, kind="real"):
        after = tuple(a for a in after if a is not None)
        n = None
        sname = eng
        if inc == "c":
            self.cnt["c"] += 1
            n = self.cnt["c"]
            sname = "c"
        elif inc:
            self.cnt[eng] += 16 if eng in DMA_ENG else 1
            n = self.cnt[eng]
        self.seq += 1
        self.items[eng].append([fn, after, n, kind, sname, self.seq])
        return (sname, n) if inc else None

    def drain(self, eng):
        self.seq += 1
        self.items[eng].append([lambda E: E.drain(fusable=False), (), None,
                                "drain", eng, self.seq])

    def dma(self, eng, out, in_, after=(), fence=True):
        if fence and self.cnt[eng] > 0:
            after = tuple(after) + ((eng, self.cnt[eng]),)
        return self.op(eng, lambda E: E.dma_start(out=out, in_=in_), after, inc=True)

    def tok(self, eng):
        return (eng, self.cnt[eng])

    def barrier(self):
        for e in ("t", "v", "s"):
            for it in reversed(self.items[e]):
                if it[3] == "real" and it[0] is not None:
                    if it[2] is None:
                        self.cnt[e] += 1
                        it[2] = self.cnt[e]
                    break
        for e in ("v", "s"):
            self.drain(e)
        snap = dict(self.cnt)
        for e in ENGS:
            waits = tuple((o, snap[o]) for o in list(ENGS) + ["c"]
                          if o != e and snap[o] > 0)
            self.seq += 1
            self.items[e].append([None, waits, None, "wait", e, self.seq])

    def emit(self, sems, block):
        lim = _EMIT_LIM

        def make(e):
            def run(E):
                wm = {}
                for fn, waits, n, _kind, sname, seq in self.items[e]:
                    if lim is not None and seq > lim:
                        break
                    for (oe, val) in waits:
                        if oe == e and e not in DMA_ENG:
                            continue
                        if wm.get(oe, 0) < val:
                            E.wait_ge(sems[oe], val)
                            wm[oe] = val
                    inst = fn(E) if fn is not None else None
                    if n is not None:
                        inst.then_inc(sems[sname],
                                      1 if sname == "c" else
                                      (16 if e in DMA_ENG else 1))
            return run
        block.tensor(make("t"))
        block.vector(make("v"))
        block.scalar(make("s"))
        block.sync(make("q"))
        block.gpsimd(make("g"))


def build_program(taps=(), stage_max=99):
    nc = bass.Bass("TRN2", target_bir_lowering=False)

    def din(name, shape, dt=F32):
        return nc.dram_tensor(name, shape, dt, kind="ExternalInput").ap()

    xfull = din("xfull", [TOK_FULL, D])
    xown = din("xown", [TOK_OWN, D])
    wq0T = din("wq0T", [D, D]); wk0T = din("wk0T", [D, D])
    wv0T = din("wv0T", [D, D]); wo0T = din("wo0T", [D, D])
    bcols0 = din("bcols0", [128, 16])
    brow_v0 = din("brow_v0", [1, D]); brow_o0 = din("brow_o0", [1, D])
    pblk_in = din("pblk", [128, 128])
    fbT_in = din("fbT", [16, 16]); vT_in = din("vT", [D, 16])
    uT_in = din("uT", [16, HID]); gateb_in = din("gateb", [128, 1])
    wfcT_in = din("wfcT", [D, HID]); wprT_in = din("wprT", [HID, D])
    wq1T = din("wq1T", [D, 512]); wk1T = din("wk1T", [D, 512])
    wv1T = din("wv1T", [D, 512])
    bcols1 = din("bcols1", [128, 8])
    brow_v1 = din("brow_v1", [1, 512])
    wo1To = din("wo1To", [512, D])
    bob_in = din("bob", [128, D])
    btiles_in = din("btiles", [128, 8 * 3 * 128])
    rbf_in = din("rbf", [128, 16])
    wgT_in = din("wgT", [D, HID]); wuT_in = din("wuT", [D, HID])
    wdT_in = din("wdT", [HID, D])
    ident_in = din("ident", [128, 128])
    identb_in = din("identb", [128, 128], BF16)
    ones_row_in = din("ones_row", [1, 128])

    yout = nc.dram_tensor("y", [TOK_OWN, D], F32, kind="ExternalOutput").ap()

    def scr(name, shape, dt=F32):
        return nc.dram_tensor(name, shape, dt, kind="Internal").ap()

    tfc_s = scr("tfc_s", [D, HID])
    tpr_s = scr("tpr_s", [HID, D])
    qs3 = scr("qs3", [4, 128, TOK_FULL])
    vps = scr("vps", [TOK_FULL, 520])
    xos = scr("xos", [TOK_OWN, D])
    gus = scr("gus", [TOK_OWN, HID])
    ccin1 = scr("ccin1", [TOK_OWN, D])
    ccout1 = scr("ccout1", [TOK_FULL, D])
    ccin2 = scr("ccin2", [TOK_FULL, D])
    ccout2 = scr("ccout2", [TOK_OWN, D])

    TAPSHAPES = {
        "x1": [TOK_OWN, D], "x2": [TOK_OWN, D], "kv": [128, 512],
        "ksum": [128, 16], "x2full": [TOK_FULL, D], "x3": [TOK_OWN, D],
        "gu": [TOK_OWN, HID], "wo1": [TOK_OWN, D], "q1": [4, 128, TOK_FULL],
        "v1": [TOK_FULL, 520], "k1": [128, 4 * TOK_FULL],
        "tfc": [D, HID], "tpr": [HID, D], "y1f": [128, 16384],
        "s0c": [128, 32], "rscb": [128, 4096],
        "s2g": [16, 2048],
    }
    tapd = {}
    for name in taps:
        tapd[name] = nc.dram_tensor("tap_" + name, TAPSHAPES[name], F32,
                                    kind="ExternalOutput").ap()

    P = Prog()

    with ExitStack() as ctx:
        KB = 1024

        SBUF_BASE = 16512

        def at(name, shape, dt, off):
            return nc.alloc_sbuf_tensor_at(name, list(shape), dt,
                                           offset=SBUF_BASE + off)

        big1 = at("big1", [128, 16384], F32, 0)
        big2 = at("big2", [128, 16384], F32, 64 * KB)
        xqfm = at("xqfm", [128, 16384], BF16, 64 * KB)
        terngb = at("terngb", [128, 4096], BF16, 96 * KB)
        ternub = at("ternub", [128, 4096], BF16, 104 * KB)
        terndb = at("terndb", [128, 32768], BF16, 64 * KB)
        chA = at("chA", [128, 4096], F32, 131072)
        tii = at("tii", [128, 512], I32, 204480)
        qsc2 = at("qsc2", [128, 512], F32, 206528)
        chB = at("chB", [128, 4352], F32, 147456)
        guqfm = at("guqfm", [128, 4096], BF16, 155648)
        chC = at("chC", [128, 4096], F32, 164864)
        chD = at("chD", [128, 4096], F32, 181248)
        kvblk = at("kvblk", [128, 1040], F32, 197632)
        btb = at("btb", [128, 3072], BF16, 181248)
        atm = at("atm", [128, 2048], F32, 187392)
        sm1 = at("sm1", [128, 1024], F32, 197632)
        kvsb = at("kvsb", [128, 512], F32, 201728)
        stats = at("stats", [128, 96], F32, 203776)
        ksumsb = at("ksumsb", [128, 16], F32, 204160)
        misc = at("misc", [128, 64], F32, 204224)
        rowb = at("rowb", [1, 1152], F32, 204480)
        ident = at("ident", [128, 128], F32, 209088)
        identb = at("identb", [128, 128], BF16, 209600)
        pblks = at("pblks", [128, 128], F32, 209856)
        bc0 = at("bc0", [128, 16], F32, 210368)
        bc1 = at("bc1", [128, 8], F32, 210432)
        rbf = at("rbf", [128, 16], F32, 210464)
        gatebs = at("gatebs", [128, 1], F32, 210528)
        fbTs = at("fbTs", [16, 16], F32, 210560)
        vTs = at("vTs", [128, 128], F32, 210624)
        onescol = at("onescol", [128, 1], F32, 211136)
        eps5c = at("eps5c", [128, 1], F32, 211168)
        s0c = at("s0c", [128, 64], F32, 211200)
        psum = ctx.enter_context(nc.psum_tensor("psum", [128, 4096], F32))

        sems = {e: ctx.enter_context(nc.semaphore("sem_" + e))
                for e in list(ENGS) + ["c"]}
        block = ctx.enter_context(nc.Block())

        ORow = rowb[0:1, 1024:1152]
        BRow = rowb[0:1, 0:1024]

        def bank(i, lo=0, hi=512):
            return psum[:, i * 512 + lo: i * 512 + hi]

        def bankp(i, p0, p1, lo=0, hi=512):
            return psum[p0:p1, i * 512 + lo: i * 512 + hi]

        def tap_sbuf(name, ap_fn, tr3=False):
            if name in tapd:
                P.barrier()
                dst = tapd[name]
                if tr3:
                    dst = dst.rearrange("(n p) d -> p n d", p=128)
                P.dma("g", dst, ap_fn())
                P.barrier()

        def tap_dram(name, src):
            if name in tapd:
                P.barrier()
                P.dma("g", tapd[name], src)
                P.barrier()

        def emit_ln(xb, ntiles, xtok):
            n = ntiles
            for tt in range(ntiles):
                xt = xb[:, tt * 1024:(tt + 1) * 1024]
                P.op("v", (lambda xt=xt, tt=tt: lambda E: E.bn_stats(
                    stats[:, tt * 12: tt * 12 + 6], xt[:, 0:512]))(),
                    after=(xtok,) if tt == 0 else ())
                P.op("v", (lambda xt=xt, tt=tt: lambda E: E.bn_stats(
                    stats[:, tt * 12 + 6: tt * 12 + 12], xt[:, 512:1024]))())
            P.drain("v")
            for tt in range(ntiles):
                P.op("v", (lambda tt=tt: lambda E: E.bn_aggr(
                    stats[:, 48 + 2 * tt: 50 + 2 * tt], stats[:, tt * 12: tt * 12 + 12]))(),
                    inc=(tt == ntiles - 1))
            P.drain("v")
            vtok = P.tok("v")
            P.op("s", (lambda n=n: lambda E: E.activation(
                stats[:, 56:56 + n], stats[:, 49:49 + 2 * n:2], AF.Sqrt,
                bias=eps5c[:]))(), after=(vtok,), inc=True)
            stok = P.tok("s")
            P.op("v", (lambda n=n: lambda E: E.reciprocal(
                stats[:, 64:64 + n], stats[:, 56:56 + n]))(), after=(stok,))
            P.drain("v")
            P.op("v", (lambda n=n: lambda E: E.tensor_tensor(
                stats[:, 72:72 + n], stats[:, 48:48 + 2 * n:2],
                stats[:, 64:64 + n], ALU.mult))())
            P.drain("v")
            P.op("v", (lambda n=n: lambda E: E.tensor_scalar_mul(
                stats[:, 72:72 + n], stats[:, 72:72 + n], -1.0))())
            P.drain("v")
            for tt in range(ntiles):
                xt = xb[:, tt * 1024:(tt + 1) * 1024]
                P.op("v", (lambda xt=xt, tt=tt: lambda E: E.tensor_scalar(
                    xt, xt, stats[:, 64 + tt:65 + tt], stats[:, 72 + tt:73 + tt],
                    ALU.mult, ALU.add))(), inc=(tt == ntiles - 1))
            P.drain("v")
            return P.tok("v")

        def emit_tr_chunk(src, dst, srctok, busytok=None):
            prev_evac = {0: None, 1: None}
            for dc in range(8):
                bk = 6 + (dc % 2)
                for tt in range(4):
                    in_ap = src[:, tt * 1024 + dc * 128: tt * 1024 + dc * 128 + 128]
                    o = psum[:, bk * 512 + tt * 128: bk * 512 + tt * 128 + 128]
                    P.op("t", (lambda o=o, in_ap=in_ap: lambda E: E.transpose(
                        o, in_ap, ident[:]))(),
                        after=(srctok, prev_evac[dc % 2], busytok) if tt == 0 else (),
                        inc=(tt == 3))
                mtok = P.tok("t")
                P.op("s", (lambda dc=dc, bk=bk: lambda E: E.copy(
                    dst[:, dc * 512:(dc + 1) * 512], bank(bk)))(), after=(mtok,), inc=True)
                prev_evac[dc % 2] = P.tok("s")
            return P.tok("s")

        def emit_quant(src, out, ncols, scol=None, bc_map=None,
                       lo=-1.0, hi=1.0, after=()):
            np_ = ncols // 512
            mode = "tern" if lo == -1.0 else "int"
            for piece in range(np_):
                sl = slice(piece * 512, (piece + 1) * 512)
                if bc_map is not None:
                    bca = bc_map(piece)
                    P.op("v", (lambda sl=sl, bca=bca: lambda E: E.tensor_tensor(
                        kvsb[:], src[:, sl], bca, ALU.mult))(),
                        after=after if piece == 0 else ())
                else:
                    shift = None if mode == "tern" else 128.5
                    if shift is None:
                        P.op("v", (lambda sl=sl: lambda E: E.tensor_scalar(
                            kvsb[:], src[:, sl], scol, None, ALU.mult))(),
                            after=after if piece == 0 else ())
                    else:
                        P.op("v", (lambda sl=sl: lambda E: E.tensor_scalar(
                            kvsb[:], src[:, sl], scol, 128.5, ALU.mult, ALU.add))(),
                            after=after if piece == 0 else ())
                P.drain("v")
                if mode == "tern":
                    if bc_map is None:
                        pass
                    P.op("v", (lambda: lambda E: E.tensor_scalar(
                        qsc2[:], kvsb[:], 0.5, None, ALU.is_ge))())
                    P.op("v", (lambda sl=sl: lambda E: E.tensor_scalar(
                        out[:, sl], kvsb[:], -0.5, None, ALU.is_le))())
                    P.drain("v")
                    P.op("v", (lambda sl=sl: lambda E: E.tensor_tensor(
                        out[:, sl], qsc2[:], out[:, sl], ALU.subtract))(),
                        inc=(piece == np_ - 1))
                    P.drain("v")
                else:
                    if bc_map is not None:
                        P.op("v", lambda E: E.tensor_scalar(
                            kvsb[:], kvsb[:], 128.5, None, ALU.add))
                        P.drain("v")
                    P.op("v", (lambda: lambda E: E.tensor_copy(tii[:], kvsb[:]))())
                    P.drain("v")
                    P.op("v", (lambda: lambda E: E.tensor_copy(qsc2[:], tii[:]))())
                    P.drain("v")
                    P.op("v", (lambda sl=sl: lambda E: E.tensor_tensor(
                        out[:, sl], kvsb[:], qsc2[:], ALU.subtract))())
                    P.drain("v")
                    P.op("v", (lambda sl=sl: lambda E: E.tensor_scalar(
                        kvsb[:], out[:, sl], 0.0, None, ALU.is_lt))())
                    P.drain("v")
                    P.op("v", (lambda: lambda E: E.tensor_tensor(
                        qsc2[:], qsc2[:], kvsb[:], ALU.subtract))())
                    P.drain("v")
                    P.op("v", (lambda sl=sl: lambda E: E.tensor_scalar(
                        kvsb[:], out[:, sl], 1.0, None, ALU.is_ge))())
                    P.drain("v")
                    P.op("v", (lambda: lambda E: E.tensor_tensor(
                        qsc2[:], qsc2[:], kvsb[:], ALU.add))())
                    P.drain("v")
                    P.op("v", (lambda: lambda E: E.tensor_scalar(
                        qsc2[:], qsc2[:], 255.0, 0.0, ALU.min, ALU.max))())
                    P.drain("v")
                    P.op("v", (lambda sl=sl: lambda E: E.tensor_scalar(
                        out[:, sl], qsc2[:], -128.0, None, ALU.add))(),
                        inc=(piece == np_ - 1))
                    P.drain("v")
            return P.tok("v")

        P.dma("q", ident[:], ident_in)
        P.dma("q", identb[:], identb_in)
        pb = P.dma("q", pblks[:], pblk_in)
        P.dma("q", bc0[:], bcols0)
        P.dma("q", bc1[:], bcols1)
        P.dma("q", rbf[:], rbf_in)
        P.dma("q", gatebs[:], gateb_in)
        P.dma("q", fbTs[:], fbT_in)
        P.dma("q", vTs[:].rearrange("p (c j) -> p c j", j=16),
              vT_in.rearrange("(c p) j -> p c j", p=128))
        P.dma("q", ORow, ones_row_in)
        P.op("v", lambda E: E.memset(onescol[:], 1.0))
        P.op("v", lambda E: E.memset(eps5c[:], 1e-5))
        P.op("v", lambda E: E.tensor_scalar_mul(pblks[:], pblks[:], PNORM),
             after=(pb,), inc=True)
        P.barrier()

        if stage_max >= 1:
            wk_t = P.dma("q", big2[:, 0:8192].rearrange("p (c o) -> p c o", o=1024),
                         wk0T.rearrange("(c p) o -> p c o", p=128))
            wv_t = P.dma("q", big2[:, 8192:16384].rearrange("p (c o) -> p c o", o=1024),
                         wv0T.rearrange("(c p) o -> p c o", p=128))

            P.op("v", lambda E: E.memset(rowb[0:1, 0:512], 0.0), inc=True)
            ztok = P.tok("v")
            P.op("t", lambda E: E.matmul(bank(0), ORow, rowb[0:1, 0:512],
                                         start=True, stop=False), after=(ztok,))
            P.op("t", lambda E: E.matmul(bank(1, 0, 16), ORow, rowb[0:1, 0:16],
                                         start=True, stop=False), inc=True)
            brv = P.dma("q", BRow, brow_v0, after=(P.tok("t"),))
            P.mark("s1_setup")

            pr_ev = {2: None, 3: None}
            kfb_ev = {4: None, 5: None}
            chunk_guard = None
            kv_last_mm = None
            for tc in range(8):
                xt = P.dma("q", chA[:].rearrange("p (n d) -> p n d", d=1024),
                           xfull.rearrange("(n p) d -> p n d", p=128)[:, tc * 4:(tc + 1) * 4, :],
                           after=(chunk_guard,))
                lnt = emit_ln(chA, 4, xt)
                P.mark("s1_c%d_ln" % tc)
                trt = emit_tr_chunk(chA, chB, lnt, busytok=chunk_guard)
                P.mark("s1_c%d_tr" % tc)
                for ot in range(8):
                    bk = 2 + ot % 2
                    aft = (trt, wk_t, pr_ev[bk], chunk_guard)
                    for dc in range(8):
                        lhs = big2[:, dc * 1024 + ot * 128:
                                   dc * 1024 + ot * 128 + 128]
                        rhs = chB[:, dc * 512:(dc + 1) * 512]
                        P.op("t", (lambda lhs=lhs, rhs=rhs, bk=bk, dc=dc: lambda E: E.matmul(
                            bank(bk), lhs, rhs, start=(dc == 0), stop=(dc == 7)))(),
                            after=aft if dc == 0 else (), inc=(dc == 7))
                    mtok = P.tok("t")
                    P.op("v", (lambda ot=ot, bk=bk: lambda E: E.tensor_scalar(
                        chC[:, ot * 512:(ot + 1) * 512], bank(bk), bc0[:, 8 + ot:9 + ot],
                        None, ALU.add))(), after=(mtok,), inc=True)
                    pr_ev[bk] = P.tok("v")
                kc_tok = P.tok("v")
                P.mark("s1_c%d_kproj" % tc)
                for tt in range(4):
                    for j in range(8):
                        bk = 4 + j // 4
                        lhs = chC[:, j * 512 + tt * 128: j * 512 + tt * 128 + 128]
                        o = psum[:, bk * 512 + (j % 4) * 128:
                                 bk * 512 + (j % 4) * 128 + 128]
                        aft = (kc_tok, kfb_ev[bk]) if j in (0, 4) else ()
                        P.op("t", (lambda o=o, lhs=lhs: lambda E: E.matmul(
                            o, lhs, pblks[:], start=True, stop=True,
                            skip_group_check=True))(),
                            after=aft, inc=(j in (3, 7)))
                        if j in (3, 7):
                            mtok = P.tok("t")
                            P.op("v", (lambda tt=tt, bk=bk, j=j: lambda E: E.tensor_scalar(
                                chD[:, tt * 1024 + (j // 4) * 512:
                                    tt * 1024 + (j // 4) * 512 + 512],
                                bank(bk), 0.0, 1e-6, ALU.max, ALU.add))(),
                                after=(mtok, kv_last_mm), inc=True)
                            kfb_ev[bk] = P.tok("v")
                kft_tok = P.tok("v")
                P.mark("s1_c%d_feats" % tc)
                for tt in range(4):
                    for oc in range(2):
                        bk = 2 + oc
                        aft = (kft_tok, wv_t, brv, pr_ev[bk])
                        for dc in range(8):
                            lhs = chB[:, dc * 512 + tt * 128: dc * 512 + tt * 128 + 128]
                            rhs = big2[:, 8192 + dc * 1024 + oc * 512:
                                       8192 + dc * 1024 + oc * 512 + 512]
                            P.op("t", (lambda lhs=lhs, rhs=rhs, bk=bk, dc=dc:
                                       lambda E: E.matmul(bank(bk), lhs, rhs,
                                                          start=(dc == 0), stop=False))(),
                                 after=aft if dc == 0 else ())
                        P.op("t", (lambda bk=bk, oc=oc: lambda E: E.matmul(
                            bank(bk), ORow, BRow[0:1, oc * 512:(oc + 1) * 512],
                            start=False, stop=True))(), inc=True)
                        mtok = P.tok("t")
                        P.op("s", (lambda tt=tt, oc=oc, bk=bk: lambda E: E.copy(
                            chC[:, tt * 1024 + oc * 512: tt * 1024 + oc * 512 + 512],
                            bank(bk)))(), after=(mtok, kv_last_mm), inc=True)
                        pr_ev[bk] = P.tok("s")
                vt_tok = P.tok("s")
                P.mark("s1_c%d_vproj" % tc)
                for tt in range(4):
                    for hh in range(16):
                        r0, c0 = 64 * (hh % 2), (hh // 2) * 64
                        lhs = chD[:, tt * 1024 + hh * 64: tt * 1024 + hh * 64 + 64]
                        rhs = chC[:, tt * 1024 + hh * 64: tt * 1024 + hh * 64 + 64]
                        o = psum[r0:r0 + 64, c0:c0 + 64]
                        P.op("t", (lambda o=o, lhs=lhs, rhs=rhs:
                                   lambda E: E.matmul(o, lhs, rhs, start=False,
                                                      stop=False, skip_group_check=True))(),
                             after=(vt_tok,) if (tt == 0 and hh == 0) else ())
                        o2 = psum[r0:r0 + 64, 512 + (hh // 2): 513 + (hh // 2)]
                        P.op("t", (lambda o2=o2, lhs=lhs:
                                   lambda E: E.matmul(o2, lhs, onescol[:], start=False,
                                                      stop=False, skip_group_check=True))(),
                             inc=(tt == 3 and hh == 15))
                kv_last_mm = P.tok("t")
                chunk_guard = kv_last_mm
                P.mark("s1_chunk%d" % tc)
            P.op("v", lambda E: E.memset(rowb[0:1, 0:512], 0.0),
                 after=(kv_last_mm,), inc=True)
            z2 = P.tok("v")
            P.op("t", lambda E: E.matmul(bank(0), ORow, rowb[0:1, 0:512],
                                         start=False, stop=True), after=(z2,))
            P.op("t", lambda E: E.matmul(bank(1, 0, 16), ORow, rowb[0:1, 0:16],
                                         start=False, stop=True), inc=True)
            kvcl = P.tok("t")
            P.op("v", lambda E: E.memset(kvblk[:], 0.0), after=(kvcl,), inc=True)
            zkb = P.tok("v")
            for j in range(8):
                P.op("s", (lambda j=j: lambda E: E.copy(
                    kvblk[0:64, j * 130: j * 130 + 64],
                    psum[0:64, j * 64:(j + 1) * 64]))(),
                    after=(zkb, kvcl) if j == 0 else ())
                P.op("s", (lambda j=j: lambda E: E.copy(
                    kvblk[64:128, j * 130 + 65: j * 130 + 129],
                    psum[64:128, j * 64:(j + 1) * 64]))())
                P.op("s", (lambda j=j: lambda E: E.copy(
                    kvblk[0:64, j * 130 + 64: j * 130 + 65],
                    psum[0:64, 512 + j: 513 + j]))())
                P.op("s", (lambda j=j: lambda E: E.copy(
                    kvblk[64:128, j * 130 + 129: j * 130 + 130],
                    psum[64:128, 512 + j: 513 + j]))(), inc=(j == 7))
            if "kv" in tapd or "ksum" in tapd:
                P.op("s", lambda E: E.copy(kvsb[:], bank(0)))
                P.op("s", lambda E: E.copy(ksumsb[:], bank(1, 0, 16)), inc=True)
            P.barrier()
            P.mark("s1_kvclose")
            tap_sbuf("kv", lambda: kvsb[:])
            tap_sbuf("ksum", lambda: ksumsb[:])

            bro = P.dma("q", BRow, brow_o0)
            wq_t = P.dma("q", big2[:, 0:8192].rearrange("p (c o) -> p c o", o=1024),
                         wq0T.rearrange("(c p) o -> p c o", p=128))
            wo_t = P.dma("q", big2[:, 8192:16384].rearrange("p (c o) -> p c o", o=1024),
                         wo0T.rearrange("(c p) o -> p c o", p=128))
            pr_ev = {2: None, 3: None, 4: None, 5: None}
            chunk_guard = None
            for oc4 in range(4):
                xt = P.dma("q", chA[:].rearrange("p (n d) -> p n d", d=1024),
                           xown.rearrange("(n p) d -> p n d", p=128)[:, oc4 * 4:(oc4 + 1) * 4, :],
                           after=(chunk_guard,))
                lnt = emit_ln(chA, 4, xt)
                trt = emit_tr_chunk(chA, chB, lnt, busytok=chunk_guard)
                for ot in range(8):
                    bk = 2 + ot % 2
                    aft = (trt, wq_t, pr_ev[bk], chunk_guard)
                    for dc in range(8):
                        lhs = big2[:, dc * 1024 + ot * 128: dc * 1024 + ot * 128 + 128]
                        rhs = chB[:, dc * 512:(dc + 1) * 512]
                        P.op("t", (lambda lhs=lhs, rhs=rhs, bk=bk, dc=dc: lambda E: E.matmul(
                            bank(bk), lhs, rhs, start=(dc == 0), stop=(dc == 7)))(),
                            after=aft if dc == 0 else (), inc=(dc == 7))
                    mtok = P.tok("t")
                    P.op("v", (lambda ot=ot, bk=bk: lambda E: E.tensor_scalar(
                        chC[:, ot * 512:(ot + 1) * 512], bank(bk), bc0[:, ot:ot + 1],
                        None, ALU.add))(), after=(mtok,), inc=True)
                    pr_ev[bk] = P.tok("v")
                q_tok = P.tok("v")
                for pr in range(8):
                    bk = 2 + pr % 2
                    P.op("t", (lambda pr=pr, bk=bk: lambda E: E.matmul(
                        bank(bk), pblks[:], chC[:, pr * 512:(pr + 1) * 512],
                        start=True, stop=True))(),
                        after=(q_tok, pr_ev[bk], chunk_guard), inc=True)
                    mtok = P.tok("t")
                    P.op("v", (lambda pr=pr, bk=bk: lambda E: E.tensor_scalar(
                        chD[:, pr * 512:(pr + 1) * 512], bank(bk), 0.0, 1e-6,
                        ALU.max, ALU.add))(), after=(mtok,), inc=True)
                    pr_ev[bk] = P.tok("v")
                qf_tok = P.tok("v")
                for tt in range(4):
                    for j in range(8):
                        bk = 4 + j % 2
                        lhs = chD[:, j * 512 + tt * 128: j * 512 + tt * 128 + 128]
                        P.op("t", (lambda lhs=lhs, bk=bk, j=j: lambda E: E.matmul(
                            bank(bk, 0, 130), lhs,
                            kvblk[:, j * 130:(j + 1) * 130],
                            start=True, stop=True, skip_group_check=True))(),
                            after=(qf_tok, pr_ev[bk]), inc=True)
                        mtok = P.tok("t")
                        P.op("v", (lambda bk=bk: lambda E: E.tensor_scalar(
                            misc[:, 32:33], bank(bk, 64, 65), 1e-6, None, ALU.add))(),
                            after=(mtok,))
                        P.op("v", (lambda bk=bk: lambda E: E.tensor_scalar(
                            misc[:, 33:34], bank(bk, 129, 130), 1e-6, None, ALU.add))())
                        P.drain("v")
                        P.op("v", lambda E: E.reciprocal(misc[:, 34:36], misc[:, 32:34]))
                        P.drain("v")
                        P.op("v", (lambda tt=tt, j=j, bk=bk: lambda E: E.tensor_scalar(
                            chA[:, tt * 1024 + j * 128: tt * 1024 + j * 128 + 64],
                            bank(bk, 0, 64), misc[:, 34:35], None, ALU.mult))())
                        P.op("v", (lambda tt=tt, j=j, bk=bk: lambda E: E.tensor_scalar(
                            chA[:, tt * 1024 + j * 128 + 64: tt * 1024 + j * 128 + 128],
                            bank(bk, 65, 129), misc[:, 35:36], None, ALU.mult))(),
                            inc=True)
                        P.drain("v")
                        pr_ev[bk] = P.tok("v")
                at_tok = P.tok("v")
                trt2 = emit_tr_chunk(chA, chB, at_tok)
                xrt = P.dma("q", chA[:].rearrange("p (n d) -> p n d", d=1024),
                            xown.rearrange("(n p) d -> p n d", p=128)[:, oc4 * 4:(oc4 + 1) * 4, :],
                            after=(trt2,))
                for tt in range(4):
                    for oc in range(2):
                        bk = 2 + oc
                        aft = (trt2, wo_t, bro, pr_ev[bk])
                        for dc in range(8):
                            lhs = chB[:, dc * 512 + tt * 128: dc * 512 + tt * 128 + 128]
                            rhs = big2[:, 8192 + dc * 1024 + oc * 512:
                                       8192 + dc * 1024 + oc * 512 + 512]
                            P.op("t", (lambda lhs=lhs, rhs=rhs, bk=bk, dc=dc:
                                       lambda E: E.matmul(bank(bk), lhs, rhs,
                                                          start=(dc == 0), stop=False))(),
                                 after=aft if dc == 0 else ())
                        P.op("t", (lambda bk=bk, oc=oc: lambda E: E.matmul(
                            bank(bk), ORow, BRow[0:1, oc * 512:(oc + 1) * 512],
                            start=False, stop=True))(), inc=True)
                        mtok = P.tok("t")
                        sl = big1[:, (oc4 * 4 + tt) * 1024 + oc * 512:
                                  (oc4 * 4 + tt) * 1024 + oc * 512 + 512]
                        P.op("v", (lambda sl=sl, tt=tt, oc=oc, bk=bk:
                                   lambda E: E.scalar_tensor_tensor(
                                       sl, bank(bk), 0.0,
                                       chA[:, tt * 1024 + oc * 512: tt * 1024 + oc * 512 + 512],
                                       ALU.bypass, ALU.add))(), after=(mtok, xrt), inc=True)
                        pr_ev[bk] = P.tok("v")
                chunk_guard = P.tok("v")
                P.mark("s1_q%d" % oc4)
            P.barrier()
            P.mark("s1_end")
            tap_sbuf("x1", lambda: big1[:].rearrange("p (n d) -> p n d", d=1024), tr3=True)

        if stage_max >= 2:
            guard = None
            for tc in range(4):
                P.op("v", (lambda tc=tc: lambda E: E.tensor_copy(
                    chA[:], big1[:, tc * 4096:(tc + 1) * 4096]))(),
                    after=(guard,), inc=True)
                ct = P.tok("v")
                P.drain("v")
                lnt = emit_ln(chA, 4, ct)
                prev_evac = {0: None, 1: None}
                for dc in range(8):
                    bk = 6 + dc % 2
                    for tt in range(4):
                        in_ap = chA[:, tt * 1024 + dc * 128: tt * 1024 + dc * 128 + 128]
                        o = psum[:, bk * 512 + tt * 128: bk * 512 + tt * 128 + 128]
                        P.op("t", (lambda o=o, in_ap=in_ap: lambda E: E.transpose(
                            o, in_ap, ident[:]))(),
                            after=(lnt, prev_evac[dc % 2]) if tt == 0 else (),
                            inc=(tt == 3))
                    mtok = P.tok("t")
                    P.op("s", (lambda dc=dc, tc=tc, bk=bk: lambda E: E.copy(
                        big2[:, dc * 2048 + tc * 512: dc * 2048 + tc * 512 + 512],
                        bank(bk)))(), after=(mtok,), inc=True)
                    prev_evac[dc % 2] = P.tok("s")
                guard = P.tok("s")
            P.barrier()
            tap_sbuf("y1f", lambda: big2[:])

            prev_t = None
            prev_a = None
            for dc in range(8):
                ld = P.dma("q", chB[:, 0:4096], wfcT_in[dc * 128:(dc + 1) * 128, :],
                           after=(prev_a,))
                P.op("s", lambda E: E.activation(chC[:], chB[:, 0:4096], AF.Abs),
                     after=(ld, prev_t), inc=True)
                atok = P.tok("s")
                prev_a = atok
                for hs in range(8):
                    P.op("t", (lambda hs=hs, dc=dc: lambda E: E.matmul(
                        psum[0:1, hs * 512:(hs + 1) * 512], onescol[:],
                        chC[:, hs * 512:(hs + 1) * 512], start=(dc == 0), stop=(dc == 7)))(),
                        after=(atok,) if hs == 0 else (), inc=(hs == 7))
                prev_t = P.tok("t")
            sum_tok = prev_t
            chB_guard = prev_a
            for sub in (3, 2, 1, 0):
                P.op("s", (lambda sub=sub: lambda E: E.copy(
                    BRow, psum[0:1, sub * 1024:(sub + 1) * 1024]))(),
                    after=(sum_tok,), inc=True)
                et = P.tok("s")
                P.op("v", lambda E: E.tensor_scalar(BRow, BRow, 1.0 / D, 1e-5,
                                                    ALU.mult, ALU.max), after=(et,), inc=True)
                P.drain("v")
                rowt = P.tok("v")
                for j in range(8):
                    P.op("t", (lambda j=j: lambda E: E.transpose(
                        psum[0:128, 3584 + j:3585 + j], rowb[0:1, j * 128:(j + 1) * 128],
                        ident[0:1, 0:1]))(),
                        after=(rowt,) if j == 0 else (), inc=(j == 7))
                trt = P.tok("t")
                P.op("s", (lambda sub=sub: lambda E: E.copy(
                    s0c[:, sub * 8: sub * 8 + 8], psum[:, 3584:3592]))(),
                    after=(trt,), inc=True)
                s0t = P.tok("s")
                P.op("v", lambda E: E.reciprocal(BRow, BRow), after=(s0t,), inc=True)
                P.drain("v")
                rect = P.tok("v")
                for half in range(2):
                    P.op("t", (lambda half=half: lambda E: E.matmul(
                        psum[:, 3072 + half * 512: 3072 + (half + 1) * 512], ORow,
                        rowb[0:1, half * 512:(half + 1) * 512], start=True, stop=True))(),
                        after=(rect,) if half == 0 else (), inc=(half == 1))
                bt = P.tok("t")
                P.op("s", (lambda sub=sub: lambda E: E.copy(
                    chD[:, sub * 1024:(sub + 1) * 1024], psum[:, 3072:4096]))(),
                    after=(bt,), inc=True)
            rscb_tok = P.tok("s")
            ut = P.dma("q", chA[0:16, 0:4096], uT_in)
            P.op("v", lambda E: E.tensor_tensor(chA[0:16, :], chA[0:16, :],
                                                chD[0:16, :], ALU.mult),
                 after=(ut, rscb_tok), inc=True)
            utg_tok = P.tok("v")
            prev_sp = None
            for hb in range(8):
                ld = P.dma("q", chB[:, 0:4096].rearrange("p (c h) -> p c h", h=512),
                           wfcT_in.rearrange("(c p) h -> p c h", p=128)[
                               :, :, hb * 512:(hb + 1) * 512],
                           after=(prev_sp, chB_guard if hb == 0 else None))
                chB_guard = None
                qt = emit_quant(chB, chC, 4096,
                                bc_map=(lambda piece, hb=hb:
                                        chD[:, hb * 512:(hb + 1) * 512]),
                                lo=-1.0, hi=1.0, after=(ld, utg_tok, prev_sp))
                prev_sp = P.dma("g", tfc_s.rearrange("(c p) h -> p c h", p=128)[
                    :, :, hb * 512:(hb + 1) * 512],
                    chC[:].rearrange("p (c h) -> p c h", h=512), after=(qt,))
                chB_guard = qt
            P.barrier()
            tap_dram("tfc", tfc_s)
            tap_sbuf("s0c", lambda: s0c[:, 0:32])
            tap_sbuf("rscb", lambda: chD[:])

            prev_t = None
            prev_a = None
            for hc in range(32):
                ld = P.dma("q", chB[:, 0:1024], wprT_in[hc * 128:(hc + 1) * 128, :],
                           after=(prev_a, chB_guard if hc == 0 else None))
                if hc == 0:
                    chB_guard = None
                P.op("s", lambda E: E.activation(chC[:, 0:1024], chB[:, 0:1024], AF.Abs),
                     after=(ld, prev_t), inc=True)
                atok = P.tok("s")
                prev_a = atok
                for hs in range(2):
                    P.op("t", (lambda hs=hs, hc=hc: lambda E: E.matmul(
                        psum[0:1, hs * 512:(hs + 1) * 512], onescol[:],
                        chC[:, hs * 512:(hs + 1) * 512], start=(hc == 0), stop=(hc == 31)))(),
                        after=(atok,) if hs == 0 else (), inc=(hs == 1))
                prev_t = P.tok("t")
            chB_guard = prev_a
            P.op("s", lambda E: E.copy(BRow, psum[0:1, 0:1024]), after=(prev_t,), inc=True)
            et = P.tok("s")
            P.op("v", lambda E: E.tensor_scalar(BRow, BRow, 1.0 / HID, 1e-5,
                                                ALU.mult, ALU.max), after=(et,), inc=True)
            P.drain("v")
            rowt = P.tok("v")
            for half in range(2):
                P.op("t", (lambda half=half: lambda E: E.matmul(
                    psum[:, 3072 + half * 512: 3072 + (half + 1) * 512], ORow,
                    rowb[0:1, half * 512:(half + 1) * 512], start=True, stop=True))(),
                    after=(rowt,) if half == 0 else (), inc=(half == 1))
            bt = P.tok("t")
            P.op("s", lambda E: E.copy(sm1[:], psum[:, 3072:4096]), after=(bt,), inc=True)
            s2bt = P.tok("s")
            P.op("v", lambda E: E.reciprocal(BRow, BRow), after=(s2bt,), inc=True)
            P.drain("v")
            rect = P.tok("v")
            for half in range(2):
                P.op("t", (lambda half=half: lambda E: E.matmul(
                    psum[:, 3072 + half * 512: 3072 + (half + 1) * 512], ORow,
                    rowb[0:1, half * 512:(half + 1) * 512], start=True, stop=True))(),
                    after=(rect,) if half == 0 else (), inc=(half == 1))
            bt = P.tok("t")
            P.op("s", lambda E: E.copy(chD[:, 0:1024], psum[:, 3072:4096]),
                 after=(bt,), inc=True)
            rpj_tok = P.tok("s")
            prev_sp = None
            for hb in range(8):
                ld = P.dma("q", chB[:, 0:4096].rearrange("p (c o) -> p c o", o=1024),
                           wprT_in.rearrange("(c p) o -> p c o", p=128)[
                               :, hb * 4:(hb + 1) * 4, :],
                           after=(prev_sp, chB_guard if hb == 0 else None))
                chB_guard = None
                qt = emit_quant(chB, chC, 4096,
                                bc_map=(lambda piece:
                                        chD[:, (piece % 2) * 512:
                                            (piece % 2) * 512 + 512]),
                                lo=-1.0, hi=1.0, after=(ld, rpj_tok, prev_sp))
                prev_sp = P.dma("g", tpr_s.rearrange("(c p) o -> p c o", p=128)[
                    :, hb * 4:(hb + 1) * 4, :],
                    chC[:].rearrange("p (c o) -> p c o", o=1024), after=(qt,))
                chB_guard = qt
            P.barrier()
            tap_dram("tpr", tpr_s)

            prev_ev = None
            for tt in range(16):
                for dc in range(8):
                    lhs = big2[:, dc * 2048 + tt * 128: dc * 2048 + tt * 128 + 128]
                    P.op("t", (lambda lhs=lhs, dc=dc: lambda E: E.matmul(
                        psum[:, 0:16], lhs, vTs[:, dc * 16:(dc + 1) * 16],
                        start=(dc == 0), stop=(dc == 7)))(),
                        after=(prev_ev,) if dc == 0 else (), inc=(dc == 7))
                mtok = P.tok("t")
                P.op("s", lambda E: E.copy(misc[:, 0:16], psum[:, 0:16]),
                     after=(mtok,), inc=True)
                et = P.tok("s")
                P.op("t", lambda E: E.transpose(psum[0:16, 512:640], misc[:, 0:16],
                                                ident[:]), after=(et,), inc=True)
                mt2 = P.tok("t")
                P.op("s", (lambda tt=tt: lambda E: E.copy(
                    chB[0:16, tt * 128:(tt + 1) * 128], psum[0:16, 512:640]))(),
                    after=(mt2, chB_guard if tt == 0 else None), inc=True)
                prev_ev = P.tok("s")
            s1_tok = prev_ev
            s2g_prev = None
            for tc in range(4):
                P.op("t", (lambda tc=tc: lambda E: E.matmul(
                    psum[0:16, 1024:1536], fbTs[:], chB[0:16, tc * 512:(tc + 1) * 512],
                    start=True, stop=True))(), after=(s1_tok, s2g_prev), inc=True)
                mtok = P.tok("t")
                P.op("v", (lambda tc=tc: lambda E: E.tensor_scalar(
                    chB[0:16, 2048 + tc * 512: 2048 + (tc + 1) * 512],
                    psum[0:16, 1024:1536], gatebs[0:16, 0:1], None, ALU.mult))(),
                    after=(mtok,), inc=True)
                s2g_prev = P.tok("v")
            s2g_tok = P.tok("v")
            P.barrier()
            tap_sbuf("s2g", lambda: chB[0:16, 2048:4096])

            fc_ev = {0: None, 1: None}
            pj_ev = {2: None, 3: None}
            h_guard = None
            tern_guard = None
            for hb in range(8):
                tld = P.dma("q", chC[:].rearrange("p (c h) -> p c h", h=512),
                            tfc_s.rearrange("(c p) h -> p c h", p=128)[
                                :, :, hb * 512:(hb + 1) * 512],
                            after=(tern_guard,))
                tld2 = P.dma("q", chD[:].rearrange("p (c o) -> p c o", o=1024),
                             tpr_s.rearrange("(c p) o -> p c o", p=128)[
                                 :, hb * 4:(hb + 1) * 4, :],
                             after=(tern_guard,))
                for tc in range(4):
                    for hs in range(4):
                        bk = hs % 2
                        aft = (tld, fc_ev[bk], s2g_tok, h_guard)
                        for dc in range(8):
                            lhs = chC[:, dc * 512 + hs * 128: dc * 512 + hs * 128 + 128]
                            rhs = big2[:, dc * 2048 + tc * 512: dc * 2048 + tc * 512 + 512]
                            P.op("t", (lambda lhs=lhs, rhs=rhs, bk=bk, dc=dc:
                                       lambda E: E.matmul(bank(bk), lhs, rhs,
                                                          start=(dc == 0), stop=False))(),
                                 after=aft if dc == 0 else ())
                        lhsu = chA[0:16, hb * 512 + hs * 128: hb * 512 + hs * 128 + 128]
                        P.op("t", (lambda lhsu=lhsu, bk=bk, tc=tc: lambda E: E.matmul(
                            bank(bk), lhsu, chB[0:16, 2048 + tc * 512: 2048 + (tc + 1) * 512],
                            start=False, stop=True))(), inc=True)
                        mtok = P.tok("t")
                        scol = s0c[:, hb * 4 + hs: hb * 4 + hs + 1]
                        P.op("v", (lambda bk=bk, scol=scol: lambda E: E.tensor_scalar(
                            kvsb[:], bank(bk), scol, 0.0, ALU.mult, ALU.max))(),
                            after=(mtok,))
                        P.drain("v")
                        P.op("v", (lambda hs=hs: lambda E: E.tensor_tensor(
                            chB[:, hs * 512:(hs + 1) * 512], kvsb[:],
                            kvsb[:], ALU.mult))(), inc=True)
                        P.drain("v")
                        fc_ev[bk] = P.tok("v")
                    h_tok = P.tok("v")
                    for tt in range(4):
                        for oc in range(2):
                            bk = 2 + oc
                            aft = (h_tok, tld2, pj_ev[bk])
                            for hs in range(4):
                                lhs = chB[:, hs * 512 + tt * 128: hs * 512 + tt * 128 + 128]
                                rhs = chD[:, hs * 1024 + oc * 512: hs * 1024 + oc * 512 + 512]
                                P.op("t", (lambda lhs=lhs, rhs=rhs, bk=bk, hs=hs:
                                           lambda E: E.matmul(bank(bk), lhs, rhs,
                                                              start=(hs == 0),
                                                              stop=(hs == 3)))(),
                                     after=aft if hs == 0 else (), inc=(hs == 3))
                            mtok = P.tok("t")
                            P.op("v", (lambda bk=bk, oc=oc: lambda E: E.tensor_tensor(
                                kvsb[:], bank(bk),
                                sm1[:, oc * 512:(oc + 1) * 512], ALU.mult))(),
                                after=(mtok,))
                            P.drain("v")
                            sl = big1[:, (tc * 4 + tt) * 1024 + oc * 512:
                                      (tc * 4 + tt) * 1024 + oc * 512 + 512]
                            P.op("v", (lambda sl=sl: lambda E: E.tensor_tensor(
                                sl, sl, kvsb[:], ALU.add))(), inc=True)
                            P.drain("v")
                            pj_ev[bk] = P.tok("v")
                    h_guard = P.tok("v")
                tern_guard = P.tok("v")
            P.barrier()
            tap_sbuf("x2", lambda: big1[:].rearrange("p (n d) -> p n d", d=1024), tr3=True)

        if stage_max >= 3:
            t1 = P.dma("g", ccin1.rearrange("(n p) d -> p n d", p=128),
                       big1[:].rearrange("p (n d) -> p n d", d=1024))
            t2 = P.dma("g", xos.rearrange("(n p) d -> p n d", p=128),
                       big1[:].rearrange("p (n d) -> p n d", d=1024))
            cc1 = P.op("g", lambda E: E.collective_compute(
                "AllGather", ALU.bypass, replica_groups=RG, ins=[ccin1], outs=[ccout1]),
                after=(t1, t2, P.tok("g")), inc="c")
            P.barrier()
            tap_dram("x2full", ccout1)

            wk_t = P.dma("q", big2[:, 0:4096].rearrange("p (c o) -> p c o", o=512),
                         wk1T.rearrange("(c p) o -> p c o", p=128))
            wv_t = P.dma("q", big2[:, 4096:8192].rearrange("p (c o) -> p c o", o=512),
                         wv1T.rearrange("(c p) o -> p c o", p=128))
            wq_t = P.dma("q", big2[:, 8192:12288].rearrange("p (c o) -> p c o", o=512),
                         wq1T.rearrange("(c p) o -> p c o", p=128))
            wo_t = P.dma("q", big2[:, 12288:16384].rearrange("p (c o) -> p c o", o=1024),
                         wo1To.rearrange("(c p) o -> p c o", p=128))
            brv = P.dma("q", BRow[0:1, 0:512], brow_v1)
            for slot in range(4):
                P.op("v", (lambda slot=slot: lambda E: E.memset(
                    chC[:, slot * 520:(slot + 1) * 520].rearrange(
                        "p (h c) -> p h c", c=65)[:, :, 64:65], 1.0))(),
                    inc=(slot == 3))
            ones_tok = P.tok("v")
            pr_ev = {2: None, 3: None}
            chunk_guard = None
            prev_sp = None
            for tc in range(8):
                xt = P.dma("q", chA[:].rearrange("p (n d) -> p n d", d=1024),
                           ccout1.rearrange("(n p) d -> p n d", p=128)[:, tc * 4:(tc + 1) * 4, :],
                           after=(chunk_guard, cc1))
                lnt = emit_ln(chA, 4, xt)
                trt = emit_tr_chunk(chA, chB[:, 0:4096], lnt, busytok=chunk_guard)
                for ot in range(4):
                    bk = 2 + ot % 2
                    aft = (trt, wk_t, pr_ev[bk])
                    for dc in range(8):
                        lhs = big2[:, dc * 512 + ot * 128: dc * 512 + ot * 128 + 128]
                        rhs = chB[:, dc * 512:(dc + 1) * 512]
                        P.op("t", (lambda lhs=lhs, rhs=rhs, bk=bk, dc=dc: lambda E: E.matmul(
                            bank(bk), lhs, rhs, start=(dc == 0), stop=(dc == 7)))(),
                            after=aft if dc == 0 else (), inc=(dc == 7))
                    mtok = P.tok("t")
                    P.op("v", (lambda ot=ot, tc=tc, bk=bk: lambda E: E.tensor_scalar(
                        big1[:, ot * 4096 + tc * 512: ot * 4096 + tc * 512 + 512],
                        bank(bk), bc1[:, 4 + ot:5 + ot], None, ALU.add))(),
                        after=(mtok,), inc=True)
                    pr_ev[bk] = P.tok("v")
                for tt in range(4):
                    bk = 2 + tt % 2
                    aft = (wv_t, brv, pr_ev[bk])
                    for dc in range(8):
                        lhs = chB[:, dc * 512 + tt * 128: dc * 512 + tt * 128 + 128]
                        rhs = big2[:, 4096 + dc * 512: 4096 + dc * 512 + 512]
                        P.op("t", (lambda lhs=lhs, rhs=rhs, bk=bk, dc=dc: lambda E: E.matmul(
                            bank(bk), lhs, rhs, start=(dc == 0), stop=False))(),
                            after=aft if dc == 0 else ())
                    P.op("t", (lambda bk=bk: lambda E: E.matmul(
                        bank(bk), ORow, BRow[0:1, 0:512], start=False, stop=True))(),
                        inc=True)
                    mtok = P.tok("t")
                    P.op("s", (lambda tt=tt, bk=bk: lambda E: E.copy(
                        chC[:, tt * 520:(tt + 1) * 520].rearrange(
                            "p (h c) -> p h c", c=65)[:, :, 0:64],
                        bank(bk).rearrange("p (h c) -> p h c", c=64)))(),
                        after=(mtok, prev_sp, ones_tok), inc=True)
                    pr_ev[bk] = P.tok("s")
                    prev_sp = P.dma(
                        "g", vps[(tc * 4 + tt) * 128:(tc * 4 + tt + 1) * 128, :],
                        chC[:, tt * 520:(tt + 1) * 520], after=(pr_ev[bk],))
                for ot in range(4):
                    bk = 2 + ot % 2
                    aft = (wq_t, pr_ev[bk])
                    for dc in range(8):
                        lhs = big2[:, 8192 + dc * 512 + ot * 128:
                                   8192 + dc * 512 + ot * 128 + 128]
                        rhs = chB[:, dc * 512:(dc + 1) * 512]
                        P.op("t", (lambda lhs=lhs, rhs=rhs, bk=bk, dc=dc: lambda E: E.matmul(
                            bank(bk), lhs, rhs, start=(dc == 0), stop=(dc == 7)))(),
                            after=aft if dc == 0 else (), inc=(dc == 7))
                    mtok = P.tok("t")
                    P.op("v", (lambda ot=ot, bk=bk: lambda E: E.tensor_scalar(
                        chD[:, ot * 512:(ot + 1) * 512], bank(bk),
                        bc1[:, ot:ot + 1], None, ALU.add))(),
                        after=(mtok, prev_sp), inc=True)
                    pr_ev[bk] = P.tok("v")
                    prev_sp = P.dma("g", qs3[ot, :, tc * 512:(tc + 1) * 512],
                                    chD[:, ot * 512:(ot + 1) * 512],
                                    after=(pr_ev[bk],))
                chunk_guard = P.tok("t")
            P.barrier()
            tap_dram("q1", qs3)
            tap_dram("v1", vps)
            tap_sbuf("k1", lambda: big1[:])

            bt_ld = P.dma("q", chC[:, 0:3072], btiles_in[:, 0:3072])
            P.op("v", lambda E: E.tensor_scalar_mul(btb[:], chC[:, 0:3072], 8.0),
                 after=(bt_ld,), inc=True)
            P.barrier()

            exp_guard = {}
            av_guard = {}
            avadd_guard = {}
            prev_fin = None
            for qc in range(8):
                qld = P.dma("q", chA[:, 0:2048].rearrange("p (r c) -> p r c", c=512),
                            qs3[:, :, qc * 512:(qc + 1) * 512].rearrange("r p c -> p r c"),
                            after=(prev_fin,))
                vld_guard = None
                for ktg in range(4):
                    vld = []
                    for g in range(8):
                        kt = ktg * 8 + g
                        vld.append(P.dma("q", chB[:, g * 520:(g + 1) * 520],
                                         vps[kt * 128:(kt + 1) * 128, :],
                                         after=(vld_guard, prev_fin) if g == 0 else ()))
                    for pr in range(4):
                        hA, hB = 2 * pr, 2 * pr + 1
                        last_av = {}
                        for g in range(8):
                            kt = ktg * 8 + g
                            par = kt % 2
                            for hh in (hA, hB):
                                bk = par * 2 + (hh % 2)
                                r0 = 64 * (hh % 2)
                                dts = [qc * 4 + j - kt for j in range(4)]
                                bands = [j for j in range(4) if abs(dts[j]) <= 1]
                                lhs = big1[r0:r0 + 64, pr * 4096 + kt * 128:
                                           pr * 4096 + kt * 128 + 128]
                                rhs = chA[r0:r0 + 64, pr * 512:(pr + 1) * 512]
                                P.op("t", (lambda lhs=lhs, rhs=rhs, bk=bk,
                                           nb=len(bands): lambda E: E.matmul(
                                               bank(bk), lhs, rhs, start=True,
                                               stop=(nb == 0), skip_group_check=True))(),
                                     after=(qld, exp_guard.get(bk)),
                                     inc=(len(bands) == 0))
                                for i, j in enumerate(bands):
                                    dt = dts[j]
                                    rb = btb[:, (hh * 3 + dt + 1) * 128:
                                             (hh * 3 + dt + 2) * 128]
                                    o = psum[:, bk * 512 + j * 128: bk * 512 + j * 128 + 128]
                                    P.op("t", (lambda o=o, rb=rb,
                                               last=(i == len(bands) - 1):
                                               lambda E: E.matmul(
                                                   o, identb[:], rb, start=False,
                                                   stop=last, skip_group_check=True))(),
                                         inc=(i == len(bands) - 1))
                                smm = P.tok("t")
                                spans = []
                                jj = 0
                                while jj < 4:
                                    cls = ("band" if abs(dts[jj]) <= 1 else
                                           ("fp" if dts[jj] >= 2 else "fm"))
                                    j2 = jj
                                    while j2 + 1 < 4 and (
                                        ("band" if abs(dts[j2 + 1]) <= 1 else
                                         ("fp" if dts[j2 + 1] >= 2 else "fm")) == cls):
                                        j2 += 1
                                    spans.append((jj, j2, cls))
                                    jj = j2 + 1
                                pslot = 2048 + par * 1024 + (hh % 2) * 512
                                for i, (j0, j1, cls) in enumerate(spans):
                                    o = chA[:, pslot + j0 * 128: pslot + (j1 + 1) * 128]
                                    iap = psum[:, bk * 512 + j0 * 128:
                                               bk * 512 + (j1 + 1) * 128]
                                    if cls == "band":
                                        bias = 0.0
                                    elif cls == "fp":
                                        bias = rbf[:, 2 * hh + 1: 2 * hh + 2]
                                    else:
                                        bias = rbf[:, 2 * hh: 2 * hh + 1]
                                    P.op("s", (lambda o=o, iap=iap, bias=bias:
                                               lambda E: E.activation(
                                                   o, iap, AF.Exp, bias=bias,
                                                   scale=0.125))(),
                                        after=(smm, av_guard.get(bk)) if i == 0 else (),
                                        inc=(i == len(spans) - 1))
                                exp_guard[bk] = P.tok("s")
                                avbk = 4 + (hh % 2)
                                lhsv = chB[:, g * 520 + hh * 65: g * 520 + hh * 65 + 65]
                                rhsp = chA[:, pslot: pslot + 512]
                                aft_av = (exp_guard[bk], vld[g],
                                          avadd_guard.get(avbk) if g == 0 else None)
                                P.op("t", (lambda lhsv=lhsv, rhsp=rhsp, avbk=avbk, g=g:
                                           lambda E: E.matmul(
                                               bankp(avbk, 0, 65), lhsv, rhsp,
                                               start=(g == 0), stop=(g == 7),
                                               skip_group_check=True))(),
                                     after=aft_av, inc=True)
                                av_guard[bk] = P.tok("t")
                                last_av[hh] = av_guard[bk]
                        if pr == 0:
                            P.drain("v")
                        for hh in (hA, hB):
                            avbk = 4 + (hh % 2)
                            sl = chC[0:65, hh * 512:(hh + 1) * 512]
                            if ktg == 0:
                                P.op("v", (lambda sl=sl, avbk=avbk: lambda E: E.tensor_copy(
                                    sl, bankp(avbk, 0, 65)))(),
                                    after=(last_av[hh],), inc=True)
                            else:
                                P.op("v", (lambda sl=sl, avbk=avbk:
                                           lambda E: E.scalar_tensor_tensor(
                                               sl, bankp(avbk, 0, 65), 0.0, sl,
                                               ALU.bypass, ALU.add))(),
                                    after=(last_av[hh],), inc=True)
                            avadd_guard[avbk] = P.tok("v")
                    vld_guard = P.tok("v")
                P.drain("v")
                av_done = P.tok("v")
                tr_ev = {6: None, 7: None}
                for hh in range(8):
                    for j in range(4):
                        bk = 6 + (hh * 4 + j) % 2
                        iap = chC[0:65, hh * 512 + j * 128: hh * 512 + j * 128 + 128]
                        P.op("t", (lambda iap=iap, bk=bk: lambda E: E.transpose(
                            bank(bk, 0, 65), iap, ident[0:65, 0:65]))(),
                            after=(av_done, tr_ev[bk]), inc=True)
                        mtok = P.tok("t")
                        P.op("v", (lambda bk=bk: lambda E: E.reciprocal(
                            misc[:, 34:35], psum[:, bk * 512 + 64: bk * 512 + 65]))(),
                            after=(mtok,))
                        P.drain("v")
                        P.op("v", (lambda bk=bk, j=j, hh=hh: lambda E: E.tensor_scalar(
                            atm[:, j * 512 + hh * 64: j * 512 + hh * 64 + 64],
                            bank(bk, 0, 64), misc[:, 34:35], None, ALU.mult))(),
                            inc=True)
                        P.drain("v")
                        tr_ev[bk] = P.tok("v")
                atm_tok = P.tok("v")
                tr2_ev = {6: None, 7: None}
                for dc in range(4):
                    bk = 6 + dc % 2
                    for j in range(4):
                        iap = atm[:, j * 512 + dc * 128: j * 512 + dc * 128 + 128]
                        o = psum[:, bk * 512 + j * 128: bk * 512 + j * 128 + 128]
                        P.op("t", (lambda o=o, iap=iap: lambda E: E.transpose(
                            o, iap, ident[:]))(),
                            after=(atm_tok, tr2_ev[bk]) if j == 0 else (), inc=(j == 3))
                    mtok = P.tok("t")
                    P.op("s", (lambda dc=dc, bk=bk: lambda E: E.copy(
                        chC[:, dc * 512:(dc + 1) * 512], bank(bk)))(),
                        after=(mtok,), inc=True)
                    tr2_ev[bk] = P.tok("s")
                fm_tok = P.tok("s")
                woe_guard = {0: None, 1: None}
                for j in range(4):
                    for oc in range(2):
                        bk = oc
                        aft = (fm_tok, wo_t, woe_guard[bk])
                        for dc in range(4):
                            lhs = chC[:, dc * 512 + j * 128: dc * 512 + j * 128 + 128]
                            rhs = big2[:, 12288 + dc * 1024 + oc * 512:
                                       12288 + dc * 1024 + oc * 512 + 512]
                            P.op("t", (lambda lhs=lhs, rhs=rhs, bk=bk, dc=dc:
                                       lambda E: E.matmul(bank(bk), lhs, rhs,
                                                          start=(dc == 0),
                                                          stop=(dc == 3)))(),
                                 after=aft if dc == 0 else (), inc=(dc == 3))
                        mtok = P.tok("t")
                        P.op("s", (lambda bk=bk, oc=oc: lambda E: E.copy(
                            chB[:, oc * 512:(oc + 1) * 512], bank(bk)))(),
                            after=(mtok,), inc=True)
                        etok = P.tok("s")
                        woe_guard[bk] = P.dma(
                            "g", ccin2[qc * 512 + j * 128: qc * 512 + j * 128 + 128,
                                       oc * 512:(oc + 1) * 512],
                            chB[:, oc * 512:(oc + 1) * 512], after=(etok,))
                prev_fin = P.tok("g")
            P.barrier()
            cc2 = P.op("g", lambda E: E.collective_compute(
                "ReduceScatter", ALU.add, replica_groups=RG, ins=[ccin2], outs=[ccout2]),
                after=(P.tok("g"),), inc="c")
            P.barrier()
            tap_dram("wo1", ccout2)
            bobt = P.dma("q", chC[:, 0:1024], bob_in)
            ld_guard = None
            for tt in range(16):
                r1 = P.dma("q", chA[:, 0:1024], ccout2[tt * 128:(tt + 1) * 128, :],
                           after=(cc2, ld_guard))
                r2 = P.dma("q", chB[:, 0:1024], xos[tt * 128:(tt + 1) * 128, :],
                           after=(ld_guard,))
                P.op("v", lambda E: E.tensor_tensor(chA[:, 0:1024], chA[:, 0:1024],
                                                    chC[:, 0:1024], ALU.add),
                     after=(r1, bobt))
                P.drain("v")
                P.op("v", (lambda tt=tt: lambda E: E.tensor_tensor(
                    big1[:, tt * 1024:(tt + 1) * 1024], chA[:, 0:1024],
                    chB[:, 0:1024], ALU.add))(), after=(r2,), inc=True)
                P.drain("v")
                ld_guard = P.tok("v")
            P.barrier()
            tap_sbuf("x3", lambda: big1[:].rearrange("p (n d) -> p n d", d=1024), tr3=True)

        if stage_max >= 4:
            guard = None
            xq_tr_guard = None
            xq_ev = {6: None, 7: None}
            for tc in range(4):
                P.op("v", (lambda tc=tc: lambda E: E.tensor_copy(
                    chA[:], big1[:, tc * 4096:(tc + 1) * 4096]))(),
                    after=(guard,), inc=True)
                ct = P.tok("v")
                P.drain("v")
                lnt = emit_ln(chA, 4, ct)
                for tt4 in range(4):
                    tt = tc * 4 + tt4
                    xt = chA[:, tt4 * 1024:(tt4 + 1) * 1024]
                    P.op("v", (lambda xt=xt: lambda E: E.tensor_reduce(
                        misc[:, 40:41], xt, AX, ALU.max, apply_absolute_value=True))(),
                        after=(lnt,))
                    P.drain("v")
                    P.op("v", (lambda tt=tt: lambda E: E.tensor_scalar_max(
                        s0c[:, 32 + tt:33 + tt], misc[:, 40:41], 1e-5))())
                    P.drain("v")
                    P.op("v", (lambda tt=tt: lambda E: E.reciprocal(
                        misc[:, 41:42], s0c[:, 32 + tt:33 + tt]))())
                    P.drain("v")
                    P.op("v", lambda E: E.tensor_scalar_mul(misc[:, 42:43],
                                                            misc[:, 41:42], 127.0), inc=True)
                    P.drain("v")
                    sct = P.tok("v")
                    qt = emit_quant(xt, chB, 1024, scol=misc[:, 42:43],
                                    lo=-128.0, hi=127.0, after=(sct, xq_tr_guard))
                    for dc in range(8):
                        bk = 6 + dc % 2
                        P.op("t", (lambda dc=dc, bk=bk: lambda E: E.transpose(
                            bank(bk, 0, 128), chB[:, dc * 128:(dc + 1) * 128],
                            ident[:]))(), after=(qt, xq_ev[bk]), inc=True)
                        mtok = P.tok("t")
                        P.op("s", (lambda dc=dc, tt=tt, bk=bk: lambda E: E.copy(
                            xqfm[:, dc * 2048 + tt * 128: dc * 2048 + tt * 128 + 128],
                            bank(bk, 0, 128)))(), after=(mtok,), inc=True)
                        xq_ev[bk] = P.tok("s")
                    xq_tr_guard = P.tok("t")
                guard = P.tok("s")
            xq_tok = guard
            P.barrier()

            def global_scale(win, outcol, nchunks, ncols, guard=None):
                prev_t2 = None
                prev_a2 = guard
                for ci in range(nchunks):
                    ld = P.dma("q", chB[:, 0:ncols], win[ci * 128:(ci + 1) * 128, :],
                               after=(prev_a2,))
                    P.op("s", (lambda ncols=ncols: lambda E: E.activation(
                        chC[:, 0:ncols], chB[:, 0:ncols], AF.Abs))(),
                        after=(ld, prev_t2), inc=True)
                    atok = P.tok("s")
                    prev_a2 = atok
                    for hs in range(ncols // 512):
                        P.op("t", (lambda hs=hs, ci=ci, nchunks=nchunks:
                                   lambda E: E.matmul(
                                       psum[0:1, hs * 512:(hs + 1) * 512], onescol[:],
                                       chC[:, hs * 512:(hs + 1) * 512],
                                       start=(ci == 0), stop=(ci == nchunks - 1)))(),
                             after=(atok,) if hs == 0 else (),
                             inc=(hs == ncols // 512 - 1))
                    prev_t2 = P.tok("t")
                P.op("v", (lambda ncols=ncols: lambda E: E.tensor_reduce(
                    misc[0:1, 44:45], psum[0:1, 0:ncols], AX, ALU.add))(),
                    after=(prev_t2,))
                P.drain("v")
                P.op("v", lambda E: E.tensor_scalar(misc[0:1, 45:46], misc[0:1, 44:45],
                                                    1.0 / (D * HID), 1e-5,
                                                    ALU.mult, ALU.max), inc=True)
                P.drain("v")
                vt_ = P.tok("v")
                P.op("t", lambda E: E.matmul(psum[:, 3072:3073], ORow, misc[0:1, 45:46],
                                             start=True, stop=True),
                     after=(vt_,), inc=True)
                mt_ = P.tok("t")
                P.op("s", (lambda outcol=outcol: lambda E: E.copy(
                    s0c[:, outcol:outcol + 1], psum[:, 3072:3073]))(),
                    after=(mt_,), inc=True)
                et_ = P.tok("s")
                P.op("v", (lambda outcol=outcol: lambda E: E.reciprocal(
                    s0c[:, outcol + 1:outcol + 2], s0c[:, outcol:outcol + 1]))(),
                    after=(et_,), inc=True)
                P.drain("v")
                return P.tok("v"), prev_a2

            g_t, g_abs = global_scale(wgT_in, 48, 8, 4096)
            u_t, u_abs = global_scale(wuT_in, 50, 8, 4096, guard=g_abs)

            guev_guard = None
            gu_spill_tok = None
            for hb in range(8):
                tgld = P.dma("q", chC[:].rearrange("p (c h) -> p c h", h=512),
                             wgT_in.rearrange("(c p) h -> p c h", p=128)[
                                 :, :, hb * 512:(hb + 1) * 512],
                             after=(guev_guard, u_t if hb == 0 else None))
                tg = emit_quant(chC, terngb, 4096, scol=s0c[:, 49:50],
                                lo=-1.0, hi=1.0, after=(tgld, g_t))
                tuld = P.dma("q", chD[:].rearrange("p (c h) -> p c h", h=512),
                             wuT_in.rearrange("(c p) h -> p c h", p=128)[
                                 :, :, hb * 512:(hb + 1) * 512],
                             after=(guev_guard, u_t if hb == 0 else None))
                tu = emit_quant(chD, ternub, 4096, scol=s0c[:, 51:52],
                                lo=-1.0, hi=1.0, after=(tuld, u_t))
                for tt in range(16):
                    for src, bk0, tok0 in ((terngb, 0, tg), (ternub, 2, tu)):
                        bk = bk0 + tt % 2
                        aft = (tok0, xq_tok, guev_guard)
                        for dc in range(8):
                            lhs = xqfm[:, dc * 2048 + tt * 128: dc * 2048 + tt * 128 + 128]
                            rhs = src[:, dc * 512:(dc + 1) * 512]
                            P.op("t", (lambda lhs=lhs, rhs=rhs, bk=bk, dc=dc:
                                       lambda E: E.matmul(bank(bk), lhs, rhs,
                                                          start=(dc == 0),
                                                          stop=(dc == 7)))(),
                                 after=aft if dc == 0 else (), inc=(dc == 7))
                    gmt = P.tok("t")
                    bkg, bku = 0 + tt % 2, 2 + tt % 2
                    P.op("v", (lambda tt=tt: lambda E: E.tensor_scalar(
                        misc[:, 46:47], s0c[:, 32 + tt:33 + tt], 1.0 / 127.0,
                        None, ALU.mult))(), after=(gmt,))
                    P.drain("v")
                    P.op("v", lambda E: E.tensor_tensor(misc[:, 47:48], misc[:, 46:47],
                                                        s0c[:, 48:49], ALU.mult))
                    P.op("v", lambda E: E.tensor_tensor(misc[:, 48:49], misc[:, 46:47],
                                                        s0c[:, 50:51], ALU.mult), inc=True)
                    P.drain("v")
                    sct = P.tok("v")
                    P.op("s", (lambda bkg=bkg: lambda E: E.activation(
                        sm1[:, 0:512], bank(bkg), AF.Silu, scale=misc[:, 47:48]))(),
                        after=(sct,), inc=True)
                    silt = P.tok("s")
                    P.op("v", (lambda bku=bku: lambda E: E.scalar_tensor_tensor(
                        sm1[:, 512:1024], bank(bku), misc[:, 48:49], sm1[:, 0:512],
                        ALU.mult, ALU.mult))(), after=(silt, gu_spill_tok), inc=True)
                    gut = P.tok("v")
                    P.drain("v")
                    P.op("v", lambda E: E.tensor_reduce(misc[:, 49:50], sm1[:, 512:1024],
                                                        AX, ALU.max,
                                                        apply_absolute_value=True))
                    P.drain("v")
                    if hb == 0:
                        P.op("v", (lambda tt=tt: lambda E: E.tensor_copy(
                            stats[:, tt:tt + 1], misc[:, 49:50]))(), inc=True)
                    else:
                        P.op("v", (lambda tt=tt: lambda E: E.tensor_tensor(
                            stats[:, tt:tt + 1], stats[:, tt:tt + 1],
                            misc[:, 49:50], ALU.max))(), inc=True)
                    gmx = P.tok("v")
                    gu_spill_tok = P.dma(
                        "g", gus.rearrange("(n p) h -> p n h", p=128)[
                            :, tt, hb * 512:(hb + 1) * 512], sm1[:, 512:1024],
                        after=(gmx,))
                    guev_guard = P.tok("g")
            P.barrier()
            tap_dram("gu", gus)

            d_t, _d_abs = global_scale(wdT_in, 52, 32, 1024)
            tq_guard = None
            for hc in range(32):
                ld = P.dma("q", chC[:, 0:1024], wdT_in[hc * 128:(hc + 1) * 128, :],
                           after=(tq_guard, d_t if hc == 0 else None))
                tq_guard = emit_quant(chC[:, 0:1024],
                                      terndb[:, hc * 1024:(hc + 1) * 1024], 1024,
                                      scol=s0c[:, 53:54], lo=-1.0, hi=1.0,
                                      after=(ld, d_t))
            tern_d = tq_guard
            P.barrier()

            out_guard = None
            gld_guard = None
            tdq_guard = None
            for tt in range(16):
                gld = P.dma("q", chD[:], gus[tt * 128:(tt + 1) * 128, :],
                            after=(gld_guard,))
                P.op("v", (lambda tt=tt: lambda E: E.tensor_scalar_max(
                    misc[:, 50:51], stats[:, tt:tt + 1], 1e-5))(), after=(gld,))
                P.drain("v")
                P.op("v", lambda E: E.reciprocal(misc[:, 51:52], misc[:, 50:51]))
                P.drain("v")
                P.op("v", lambda E: E.tensor_scalar_mul(misc[:, 52:53],
                                                        misc[:, 51:52], 127.0))
                P.op("v", lambda E: E.tensor_scalar(misc[:, 53:54], misc[:, 50:51],
                                                    1.0 / 127.0, None, ALU.mult))
                P.drain("v")
                P.op("v", lambda E: E.tensor_tensor(misc[:, 54:55], misc[:, 53:54],
                                                    s0c[:, 52:53], ALU.mult), inc=True)
                P.drain("v")
                sct = P.tok("v")
                qt = emit_quant(chD, chC, 4096, scol=misc[:, 52:53],
                                lo=-128.0, hi=127.0, after=(sct, tdq_guard))
                ev2 = {6: None, 7: None}
                for hc in range(32):
                    bk = 6 + hc % 2
                    P.op("t", (lambda hc=hc, bk=bk: lambda E: E.transpose(
                        bank(bk, 0, 128), chC[:, hc * 128:(hc + 1) * 128],
                        ident[:]))(), after=(qt, ev2[bk]), inc=True)
                    mtok = P.tok("t")
                    P.op("s", (lambda hc=hc, bk=bk: lambda E: E.copy(
                        guqfm[:, hc * 128:(hc + 1) * 128], bank(bk, 0, 128)))(),
                        after=(mtok,), inc=True)
                    ev2[bk] = P.tok("s")
                gq_tok = P.tok("s")
                tdq_guard = P.tok("t")
                for oc in range(2):
                    bk = oc
                    aft = (gq_tok, tern_d, out_guard)
                    for hc in range(32):
                        lhs = guqfm[:, hc * 128:(hc + 1) * 128]
                        rhs = terndb[:, hc * 1024 + oc * 512: hc * 1024 + oc * 512 + 512]
                        P.op("t", (lambda lhs=lhs, rhs=rhs, bk=bk, hc=hc:
                                   lambda E: E.matmul(bank(bk), lhs, rhs,
                                                      start=(hc == 0),
                                                      stop=(hc == 31)))(),
                             after=aft if hc == 0 else (), inc=(hc == 31))
                    mtok = P.tok("t")
                    sl = big1[:, tt * 1024 + oc * 512: tt * 1024 + oc * 512 + 512]
                    P.op("v", (lambda sl=sl, bk=bk: lambda E: E.scalar_tensor_tensor(
                        sl, bank(bk), misc[:, 54:55], sl, ALU.mult, ALU.add))(),
                        after=(mtok,), inc=True)
                    out_guard = P.tok("v")
                    P.dma("g", yout[tt * 128:(tt + 1) * 128, oc * 512:(oc + 1) * 512],
                          sl, after=(out_guard,))
                gld_guard = P.tok("g")
            P.barrier()

        P.emit(sems, block)
    return nc


_NC_CACHE = {}


def _get_nc(taps, stage_max):
    key = (tuple(taps), stage_max)
    if key not in _NC_CACHE:
        _NC_CACHE[key] = build_program(taps=taps, stage_max=stage_max)
    return _NC_CACHE[key]


def kernel(**inputs):
    f32 = np.float32
    x = np.asarray(inputs["L0_attn_x"], f32)

    def T(a):
        return np.ascontiguousarray(np.asarray(a, f32).T)

    wq0T = T(inputs["L0_attn_Wq"]); wk0T = T(inputs["L0_attn_Wk"])
    wv0T = T(inputs["L0_attn_Wv"]); wo0T = T(inputs["L0_attn_Wo"])
    bq0 = np.asarray(inputs["L0_attn_bq"], f32)
    bk0 = np.asarray(inputs["L0_attn_bk"], f32)
    bv0 = np.asarray(inputs["L0_attn_bv"], f32)
    bo0 = np.asarray(inputs["L0_attn_bo"], f32)
    P0 = np.asarray(inputs["L0_attn_P"], f32)
    fast_b = np.asarray(inputs["L0_ffn_fast_b"], f32)
    wfcT = T(inputs["L0_ffn_w_fc"]); wprT = T(inputs["L0_ffn_w_proj"])
    uT = T(inputs["L0_ffn_u"]); vT = T(inputs["L0_ffn_v"])
    gate = np.asarray(inputs["L0_ffn_gate"], f32)
    wq1Tf = T(inputs["L1_attn_Wq"]); wk1Tf = T(inputs["L1_attn_Wk"])
    wv1Tf = T(inputs["L1_attn_Wv"]); wo1Tf = T(inputs["L1_attn_Wo"])
    bq1 = np.asarray(inputs["L1_attn_bq"], f32)
    bk1 = np.asarray(inputs["L1_attn_bk"], f32)
    bv1 = np.asarray(inputs["L1_attn_bv"], f32)
    bo1 = np.asarray(inputs["L1_attn_bo"], f32)
    relb = np.asarray(inputs["L1_attn_rel_bias"], f32)
    wgT = T(inputs["L1_ffn_w_gate"]); wuT = T(inputs["L1_ffn_w_up"])
    wdT = T(inputs["L1_ffn_w_down"])

    pblk = np.zeros((128, 128), f32)
    pblk[0:64, 0:64] = P0.T
    pblk[64:128, 64:128] = P0.T

    a = np.arange(128)
    bbx = np.arange(128)
    idx = {}
    for dt in (-1, 0, 1):
        rel = 128 * dt + a[None, :] - bbx[:, None]
        idx[dt] = np.clip(rel, -128, 128) + 128

    import ml_dtypes
    ident = np.eye(128, dtype=f32)
    identb = np.eye(128, dtype=ml_dtypes.bfloat16)
    ones_row = np.ones((1, 128), f32)

    taps = tuple(_TAPS)
    nc = _get_nc(taps, _STAGE_MAX)

    in_maps = []
    for c in range(8):
        b = c // 2
        hf = c % 2
        lo, hi = hf * 512, hf * 512 + 512
        heads = range(hf * 8, hf * 8 + 8)
        bt = np.empty((128, 8 * 3 * 128), f32)
        for i, h in enumerate(heads):
            for j, dt in enumerate((-1, 0, 1)):
                bt[:, (i * 3 + j) * 128:(i * 3 + j + 1) * 128] = relb[h][idx[dt]]
        rbfa = np.empty((128, 16), f32)
        for i, h in enumerate(heads):
            rbfa[:, 2 * i] = relb[h, 0]
            rbfa[:, 2 * i + 1] = relb[h, 256]
        im = {
            "xfull": np.ascontiguousarray(x[b]),
            "xown": np.ascontiguousarray(x[b, hf * 2048:(hf + 1) * 2048]),
            "wq0T": wq0T, "wk0T": wk0T, "wv0T": wv0T, "wo0T": wo0T,
            "bcols0": np.ascontiguousarray(np.concatenate(
                [bq0.reshape(8, 128).T, bk0.reshape(8, 128).T], axis=1)),
            "brow_v0": bv0.reshape(1, -1), "brow_o0": bo0.reshape(1, -1),
            "pblk": pblk, "fbT": np.ascontiguousarray(fast_b.T),
            "vT": vT, "uT": uT,
            "gateb": np.broadcast_to(gate.reshape(1, 1), (128, 1)).copy(),
            "wfcT": wfcT, "wprT": wprT,
            "wq1T": np.ascontiguousarray(wq1Tf[:, lo:hi]),
            "wk1T": np.ascontiguousarray(wk1Tf[:, lo:hi]),
            "wv1T": np.ascontiguousarray(wv1Tf[:, lo:hi]),
            "bcols1": np.ascontiguousarray(np.concatenate(
                [bq1[lo:hi].reshape(4, 128).T, bk1[lo:hi].reshape(4, 128).T], axis=1)),
            "brow_v1": bv1[lo:hi].reshape(1, -1),
            "wo1To": np.ascontiguousarray(wo1Tf[lo:hi, :]),
            "bob": np.broadcast_to(bo1.reshape(1, -1), (128, 1024)).copy(),
            "btiles": bt, "rbf": rbfa,
            "wgT": wgT, "wuT": wuT, "wdT": wdT,
            "ident": ident, "identb": identb, "ones_row": ones_row,
        }
        in_maps.append(im)

    res = run_bass_kernel_spmd(nc, in_maps, core_ids=list(range(8)))
    if taps:
        kernel._tap_results = [
            {k: v for k, v in r.items() if k.startswith("tap_")} for r in res.results]
    out = np.empty((4, 4096, 1024), np.float32)
    for c in range(8):
        b, hf = c // 2, c % 2
        out[b, hf * 2048:(hf + 1) * 2048] = res.results[c]["y"]
    return out


# version 03 (final)
# speedup vs baseline: 1.0362x; 1.0362x over previous
import math
from contextlib import ExitStack

import numpy as np
import concourse.bass as bass
import concourse.mybir as mybir
from concourse.bass_utils import run_bass_kernel_spmd

F32 = mybir.dt.float32
BF16 = mybir.dt.bfloat16
I32 = mybir.dt.int32
AF = mybir.ActivationFunctionType
ALU = mybir.AluOpType
AX = mybir.AxisListType.X

D = 1024
HID = 4096
TOK_OWN = 2048
TOK_FULL = 4096
PNORM = 1.0 / math.sqrt(math.sqrt(64.0))
RG = [[0, 1], [2, 3], [4, 5], [6, 7]]

DMA_ENG = ("q", "g")
ENGS = ("t", "v", "s", "q", "g")

_TAPS = ()
_STAGE_MAX = 99
_EMIT_LIM = None


class Prog:
    def __init__(self):
        self.items = {e: [] for e in ENGS}
        self.cnt = {e: 0 for e in ENGS}
        self.cnt["c"] = 0
        self.seq = 0
        self.marks = []

    def mark(self, label):
        self.marks.append((label, self.seq))

    def op(self, eng, fn, after=(), inc=False, kind="real"):
        after = tuple(a for a in after if a is not None)
        n = None
        sname = eng
        if inc == "c":
            self.cnt["c"] += 1
            n = self.cnt["c"]
            sname = "c"
        elif inc:
            self.cnt[eng] += 16 if eng in DMA_ENG else 1
            n = self.cnt[eng]
        self.seq += 1
        self.items[eng].append([fn, after, n, kind, sname, self.seq])
        return (sname, n) if inc else None

    def drain(self, eng):
        self.seq += 1
        self.items[eng].append([lambda E: E.drain(fusable=False), (), None,
                                "drain", eng, self.seq])

    def dma(self, eng, out, in_, after=(), fence=True):
        if fence and self.cnt[eng] > 0:
            after = tuple(after) + ((eng, self.cnt[eng]),)
        return self.op(eng, lambda E: E.dma_start(out=out, in_=in_), after, inc=True)

    def tok(self, eng):
        return (eng, self.cnt[eng])

    def barrier(self):
        for e in ("t", "v", "s"):
            for it in reversed(self.items[e]):
                if it[3] == "real" and it[0] is not None:
                    if it[2] is None:
                        self.cnt[e] += 1
                        it[2] = self.cnt[e]
                    break
        for e in ("v", "s"):
            self.drain(e)
        snap = dict(self.cnt)
        for e in ENGS:
            waits = tuple((o, snap[o]) for o in list(ENGS) + ["c"]
                          if o != e and snap[o] > 0)
            self.seq += 1
            self.items[e].append([None, waits, None, "wait", e, self.seq])

    def emit(self, sems, block):
        lim = _EMIT_LIM

        def make(e):
            def run(E):
                wm = {}
                for fn, waits, n, _kind, sname, seq in self.items[e]:
                    if lim is not None and seq > lim:
                        break
                    for (oe, val) in waits:
                        if oe == e and e not in DMA_ENG:
                            continue
                        if wm.get(oe, 0) < val:
                            E.wait_ge(sems[oe], val)
                            wm[oe] = val
                    inst = fn(E) if fn is not None else None
                    if n is not None:
                        inst.then_inc(sems[sname],
                                      1 if sname == "c" else
                                      (16 if e in DMA_ENG else 1))
            return run
        block.tensor(make("t"))
        block.vector(make("v"))
        block.scalar(make("s"))
        block.sync(make("q"))
        block.gpsimd(make("g"))


def build_program(taps=(), stage_max=99):
    nc = bass.Bass("TRN2", target_bir_lowering=False)

    def din(name, shape, dt=F32):
        return nc.dram_tensor(name, shape, dt, kind="ExternalInput").ap()

    xfull = din("xfull", [TOK_FULL, D])
    xown = din("xown", [TOK_OWN, D])
    wq0T = din("wq0T", [D, D]); wk0T = din("wk0T", [D, D])
    wv0T = din("wv0T", [D, D]); wo0T = din("wo0T", [D, D])
    bcols0 = din("bcols0", [128, 16])
    brow_v0 = din("brow_v0", [1, D]); brow_o0 = din("brow_o0", [1, D])
    pblk_in = din("pblk", [128, 128])
    fbT_in = din("fbT", [16, 16]); vT_in = din("vT", [D, 16])
    uT_in = din("uT", [16, HID]); gateb_in = din("gateb", [128, 1])
    wfcT_in = din("wfcT", [D, HID]); wprT_in = din("wprT", [HID, D])
    wq1T = din("wq1T", [D, 512]); wk1T = din("wk1T", [D, 512])
    wv1T = din("wv1T", [D, 512])
    bcols1 = din("bcols1", [128, 8])
    brow_v1 = din("brow_v1", [1, 512])
    wo1To = din("wo1To", [512, D])
    bob_in = din("bob", [128, D])
    btiles_in = din("btiles", [128, 8 * 3 * 128])
    rbf_in = din("rbf", [128, 16])
    wgT_in = din("wgT", [D, HID]); wuT_in = din("wuT", [D, HID])
    wdT_in = din("wdT", [HID, D])
    ident_in = din("ident", [128, 128])
    identb_in = din("identb", [128, 128], BF16)
    ones_row_in = din("ones_row", [1, 128])

    yout = nc.dram_tensor("y", [TOK_OWN, D], F32, kind="ExternalOutput").ap()

    def scr(name, shape, dt=F32):
        return nc.dram_tensor(name, shape, dt, kind="Internal").ap()

    tfc_s = scr("tfc_s", [D, HID])
    tpr_s = scr("tpr_s", [HID, D])
    qs3 = scr("qs3", [4, 128, TOK_FULL])
    vps = scr("vps", [TOK_FULL, 520])
    xos = scr("xos", [TOK_OWN, D])
    gus = scr("gus", [TOK_OWN, HID])
    ccin1 = scr("ccin1", [TOK_OWN, D])
    ccout1 = scr("ccout1", [TOK_FULL, D])
    ccin2 = scr("ccin2", [TOK_FULL, D])
    ccout2 = scr("ccout2", [TOK_OWN, D])
    kvrs_in = scr("kvrs_in", [256, 528])
    kvrs_out = scr("kvrs_out", [128, 528])

    TAPSHAPES = {
        "x1": [TOK_OWN, D], "x2": [TOK_OWN, D], "kv": [128, 512],
        "ksum": [128, 16], "x2full": [TOK_FULL, D], "x3": [TOK_OWN, D],
        "gu": [TOK_OWN, HID], "wo1": [TOK_OWN, D], "q1": [4, 128, TOK_FULL],
        "v1": [TOK_FULL, 520], "k1": [128, 4 * TOK_FULL],
        "tfc": [D, HID], "tpr": [HID, D], "y1f": [128, 16384],
        "s0c": [128, 32], "rscb": [128, 4096],
        "s2g": [16, 2048],
    }
    tapd = {}
    for name in taps:
        tapd[name] = nc.dram_tensor("tap_" + name, TAPSHAPES[name], F32,
                                    kind="ExternalOutput").ap()

    P = Prog()

    with ExitStack() as ctx:
        KB = 1024

        SBUF_BASE = 16512

        def at(name, shape, dt, off):
            return nc.alloc_sbuf_tensor_at(name, list(shape), dt,
                                           offset=SBUF_BASE + off)

        big1 = at("big1", [128, 16384], F32, 0)
        big2 = at("big2", [128, 16384], F32, 64 * KB)
        xqfm = at("xqfm", [128, 16384], BF16, 64 * KB)
        terngb = at("terngb", [128, 4096], BF16, 96 * KB)
        ternub = at("ternub", [128, 4096], BF16, 104 * KB)
        terndb = at("terndb", [128, 32768], BF16, 64 * KB)
        chA = at("chA", [128, 4096], F32, 131072)
        tii = at("tii", [128, 512], I32, 204480)
        qsc2 = at("qsc2", [128, 512], F32, 206528)
        chB = at("chB", [128, 4352], F32, 147456)
        guqfm = at("guqfm", [128, 4096], BF16, 155648)
        chC = at("chC", [128, 4096], F32, 164864)
        chD = at("chD", [128, 4096], F32, 181248)
        kvblk = at("kvblk", [128, 1040], F32, 197632)
        btb = at("btb", [128, 3072], BF16, 181248)
        atm = at("atm", [128, 2048], F32, 187392)
        sm1 = at("sm1", [128, 1024], F32, 197632)
        kvsb = at("kvsb", [128, 512], F32, 201728)
        stats = at("stats", [128, 96], F32, 203776)
        ksumsb = at("ksumsb", [128, 16], F32, 204160)
        misc = at("misc", [128, 64], F32, 204224)
        rowb = at("rowb", [1, 1152], F32, 204480)
        ident = at("ident", [128, 128], F32, 209088)
        identb = at("identb", [128, 128], BF16, 209600)
        pblks = at("pblks", [128, 128], F32, 209856)
        bc0 = at("bc0", [128, 16], F32, 210368)
        bc1 = at("bc1", [128, 8], F32, 210432)
        rbf = at("rbf", [128, 16], F32, 210464)
        gatebs = at("gatebs", [128, 1], F32, 210528)
        fbTs = at("fbTs", [16, 16], F32, 210560)
        vTs = at("vTs", [128, 128], F32, 210624)
        onescol = at("onescol", [128, 1], F32, 211136)
        eps5c = at("eps5c", [128, 1], F32, 211168)
        s0c = at("s0c", [128, 64], F32, 211200)
        psum = ctx.enter_context(nc.psum_tensor("psum", [128, 4096], F32))

        sems = {e: ctx.enter_context(nc.semaphore("sem_" + e))
                for e in list(ENGS) + ["c"]}
        block = ctx.enter_context(nc.Block())

        ORow = rowb[0:1, 1024:1152]
        BRow = rowb[0:1, 0:1024]

        def bank(i, lo=0, hi=512):
            return psum[:, i * 512 + lo: i * 512 + hi]

        def bankp(i, p0, p1, lo=0, hi=512):
            return psum[p0:p1, i * 512 + lo: i * 512 + hi]

        def tap_sbuf(name, ap_fn, tr3=False):
            if name in tapd:
                P.barrier()
                dst = tapd[name]
                if tr3:
                    dst = dst.rearrange("(n p) d -> p n d", p=128)
                P.dma("g", dst, ap_fn())
                P.barrier()

        def tap_dram(name, src):
            if name in tapd:
                P.barrier()
                P.dma("g", tapd[name], src)
                P.barrier()

        def emit_ln(xb, ntiles, xtok):
            n = ntiles
            for tt in range(ntiles):
                xt = xb[:, tt * 1024:(tt + 1) * 1024]
                P.op("v", (lambda xt=xt, tt=tt: lambda E: E.bn_stats(
                    stats[:, tt * 12: tt * 12 + 6], xt[:, 0:512]))(),
                    after=(xtok,) if tt == 0 else ())
                P.op("v", (lambda xt=xt, tt=tt: lambda E: E.bn_stats(
                    stats[:, tt * 12 + 6: tt * 12 + 12], xt[:, 512:1024]))())
            P.drain("v")
            for tt in range(ntiles):
                P.op("v", (lambda tt=tt: lambda E: E.bn_aggr(
                    stats[:, 48 + 2 * tt: 50 + 2 * tt], stats[:, tt * 12: tt * 12 + 12]))(),
                    inc=(tt == ntiles - 1))
            P.drain("v")
            vtok = P.tok("v")
            P.op("s", (lambda n=n: lambda E: E.activation(
                stats[:, 56:56 + n], stats[:, 49:49 + 2 * n:2], AF.Sqrt,
                bias=eps5c[:]))(), after=(vtok,), inc=True)
            stok = P.tok("s")
            P.op("v", (lambda n=n: lambda E: E.reciprocal(
                stats[:, 64:64 + n], stats[:, 56:56 + n]))(), after=(stok,))
            P.drain("v")
            P.op("v", (lambda n=n: lambda E: E.tensor_tensor(
                stats[:, 72:72 + n], stats[:, 48:48 + 2 * n:2],
                stats[:, 64:64 + n], ALU.mult))())
            P.drain("v")
            P.op("v", (lambda n=n: lambda E: E.tensor_scalar_mul(
                stats[:, 72:72 + n], stats[:, 72:72 + n], -1.0))())
            P.drain("v")
            for tt in range(ntiles):
                xt = xb[:, tt * 1024:(tt + 1) * 1024]
                P.op("v", (lambda xt=xt, tt=tt: lambda E: E.tensor_scalar(
                    xt, xt, stats[:, 64 + tt:65 + tt], stats[:, 72 + tt:73 + tt],
                    ALU.mult, ALU.add))(), inc=(tt == ntiles - 1))
            P.drain("v")
            return P.tok("v")

        def emit_tr_chunk(src, dst, srctok, busytok=None):
            prev_evac = {0: None, 1: None}
            for dc in range(8):
                bk = 6 + (dc % 2)
                for tt in range(4):
                    in_ap = src[:, tt * 1024 + dc * 128: tt * 1024 + dc * 128 + 128]
                    o = psum[:, bk * 512 + tt * 128: bk * 512 + tt * 128 + 128]
                    P.op("t", (lambda o=o, in_ap=in_ap: lambda E: E.transpose(
                        o, in_ap, ident[:]))(),
                        after=(srctok, prev_evac[dc % 2], busytok) if tt == 0 else (),
                        inc=(tt == 3))
                mtok = P.tok("t")
                P.op("s", (lambda dc=dc, bk=bk: lambda E: E.copy(
                    dst[:, dc * 512:(dc + 1) * 512], bank(bk)))(), after=(mtok,), inc=True)
                prev_evac[dc % 2] = P.tok("s")
            return P.tok("s")

        def emit_quant(src, out, ncols, scol=None, bc_map=None,
                       lo=-1.0, hi=1.0, after=()):
            np_ = ncols // 512
            mode = "tern" if lo == -1.0 else "int"
            for piece in range(np_):
                sl = slice(piece * 512, (piece + 1) * 512)
                if bc_map is not None:
                    bca = bc_map(piece)
                    P.op("v", (lambda sl=sl, bca=bca: lambda E: E.tensor_tensor(
                        kvsb[:], src[:, sl], bca, ALU.mult))(),
                        after=after if piece == 0 else ())
                else:
                    shift = None if mode == "tern" else 128.5
                    if shift is None:
                        P.op("v", (lambda sl=sl: lambda E: E.tensor_scalar(
                            kvsb[:], src[:, sl], scol, None, ALU.mult))(),
                            after=after if piece == 0 else ())
                    else:
                        P.op("v", (lambda sl=sl: lambda E: E.tensor_scalar(
                            kvsb[:], src[:, sl], scol, 128.5, ALU.mult, ALU.add))(),
                            after=after if piece == 0 else ())
                P.drain("v")
                if mode == "tern":
                    if bc_map is None:
                        pass
                    P.op("v", (lambda: lambda E: E.tensor_scalar(
                        qsc2[:], kvsb[:], 0.5, None, ALU.is_ge))())
                    P.op("v", (lambda sl=sl: lambda E: E.tensor_scalar(
                        out[:, sl], kvsb[:], -0.5, None, ALU.is_le))())
                    P.drain("v")
                    P.op("v", (lambda sl=sl: lambda E: E.tensor_tensor(
                        out[:, sl], qsc2[:], out[:, sl], ALU.subtract))(),
                        inc=(piece == np_ - 1))
                    P.drain("v")
                else:
                    if bc_map is not None:
                        P.op("v", lambda E: E.tensor_scalar(
                            kvsb[:], kvsb[:], 128.5, None, ALU.add))
                        P.drain("v")
                    P.op("v", (lambda: lambda E: E.tensor_copy(tii[:], kvsb[:]))())
                    P.drain("v")
                    P.op("v", (lambda: lambda E: E.tensor_copy(qsc2[:], tii[:]))())
                    P.drain("v")
                    P.op("v", (lambda sl=sl: lambda E: E.tensor_tensor(
                        out[:, sl], kvsb[:], qsc2[:], ALU.subtract))())
                    P.drain("v")
                    P.op("v", (lambda sl=sl: lambda E: E.tensor_scalar(
                        kvsb[:], out[:, sl], 0.0, None, ALU.is_lt))())
                    P.drain("v")
                    P.op("v", (lambda: lambda E: E.tensor_tensor(
                        qsc2[:], qsc2[:], kvsb[:], ALU.subtract))())
                    P.drain("v")
                    P.op("v", (lambda sl=sl: lambda E: E.tensor_scalar(
                        kvsb[:], out[:, sl], 1.0, None, ALU.is_ge))())
                    P.drain("v")
                    P.op("v", (lambda: lambda E: E.tensor_tensor(
                        qsc2[:], qsc2[:], kvsb[:], ALU.add))())
                    P.drain("v")
                    P.op("v", (lambda: lambda E: E.tensor_scalar(
                        qsc2[:], qsc2[:], 255.0, 0.0, ALU.min, ALU.max))())
                    P.drain("v")
                    P.op("v", (lambda sl=sl: lambda E: E.tensor_scalar(
                        out[:, sl], qsc2[:], -128.0, None, ALU.add))(),
                        inc=(piece == np_ - 1))
                    P.drain("v")
            return P.tok("v")

        P.dma("q", ident[:], ident_in)
        P.dma("q", identb[:], identb_in)
        pb = P.dma("q", pblks[:], pblk_in)
        P.dma("q", bc0[:], bcols0)
        P.dma("q", bc1[:], bcols1)
        P.dma("q", rbf[:], rbf_in)
        P.dma("q", gatebs[:], gateb_in)
        P.dma("q", fbTs[:], fbT_in)
        P.dma("q", vTs[:].rearrange("p (c j) -> p c j", j=16),
              vT_in.rearrange("(c p) j -> p c j", p=128))
        P.dma("q", ORow, ones_row_in)
        P.op("v", lambda E: E.memset(onescol[:], 1.0))
        P.op("v", lambda E: E.memset(eps5c[:], 1e-5))
        P.op("v", lambda E: E.tensor_scalar_mul(pblks[:], pblks[:], PNORM),
             after=(pb,), inc=True)
        P.barrier()

        if stage_max >= 1:
            wk_t = P.dma("q", big2[:, 0:8192].rearrange("p (c o) -> p c o", o=1024),
                         wk0T.rearrange("(c p) o -> p c o", p=128))
            wv_t = P.dma("q", big2[:, 8192:16384].rearrange("p (c o) -> p c o", o=1024),
                         wv0T.rearrange("(c p) o -> p c o", p=128))

            P.op("v", lambda E: E.memset(rowb[0:1, 0:512], 0.0), inc=True)
            ztok = P.tok("v")
            P.op("t", lambda E: E.matmul(bank(0), ORow, rowb[0:1, 0:512],
                                         start=True, stop=False), after=(ztok,))
            P.op("t", lambda E: E.matmul(bank(1, 0, 16), ORow, rowb[0:1, 0:16],
                                         start=True, stop=False), inc=True)
            brv = P.dma("q", BRow, brow_v0, after=(P.tok("t"),))
            P.mark("s1_setup")

            pr_ev = {2: None, 3: None}
            kfb_ev = {4: None, 5: None}
            chunk_guard = None
            kv_last_mm = None
            for tc in range(4):
                xt = P.dma("q", chA[:].rearrange("p (n d) -> p n d", d=1024),
                           xown.rearrange("(n p) d -> p n d", p=128)[:, tc * 4:(tc + 1) * 4, :],
                           after=(chunk_guard,))
                lnt = emit_ln(chA, 4, xt)
                P.mark("s1_c%d_ln" % tc)
                trt = emit_tr_chunk(chA, chB, lnt, busytok=chunk_guard)
                P.mark("s1_c%d_tr" % tc)
                for ot in range(8):
                    bk = 2 + ot % 2
                    aft = (trt, wk_t, pr_ev[bk], chunk_guard)
                    for dc in range(8):
                        lhs = big2[:, dc * 1024 + ot * 128:
                                   dc * 1024 + ot * 128 + 128]
                        rhs = chB[:, dc * 512:(dc + 1) * 512]
                        P.op("t", (lambda lhs=lhs, rhs=rhs, bk=bk, dc=dc: lambda E: E.matmul(
                            bank(bk), lhs, rhs, start=(dc == 0), stop=(dc == 7)))(),
                            after=aft if dc == 0 else (), inc=(dc == 7))
                    mtok = P.tok("t")
                    P.op("v", (lambda ot=ot, bk=bk: lambda E: E.tensor_scalar(
                        chC[:, ot * 512:(ot + 1) * 512], bank(bk), bc0[:, 8 + ot:9 + ot],
                        None, ALU.add))(), after=(mtok,), inc=True)
                    pr_ev[bk] = P.tok("v")
                kc_tok = P.tok("v")
                P.mark("s1_c%d_kproj" % tc)
                for tt in range(4):
                    for j in range(8):
                        bk = 4 + j // 4
                        lhs = chC[:, j * 512 + tt * 128: j * 512 + tt * 128 + 128]
                        o = psum[:, bk * 512 + (j % 4) * 128:
                                 bk * 512 + (j % 4) * 128 + 128]
                        aft = (kc_tok, kfb_ev[bk]) if j in (0, 4) else ()
                        P.op("t", (lambda o=o, lhs=lhs: lambda E: E.matmul(
                            o, lhs, pblks[:], start=True, stop=True,
                            skip_group_check=True))(),
                            after=aft, inc=(j in (3, 7)))
                        if j in (3, 7):
                            mtok = P.tok("t")
                            P.op("v", (lambda tt=tt, bk=bk, j=j: lambda E: E.tensor_scalar(
                                chD[:, tt * 1024 + (j // 4) * 512:
                                    tt * 1024 + (j // 4) * 512 + 512],
                                bank(bk), 0.0, 1e-6, ALU.max, ALU.add))(),
                                after=(mtok, kv_last_mm), inc=True)
                            kfb_ev[bk] = P.tok("v")
                kft_tok = P.tok("v")
                P.mark("s1_c%d_feats" % tc)
                for tt in range(4):
                    for oc in range(2):
                        bk = 2 + oc
                        aft = (kft_tok, wv_t, brv, pr_ev[bk])
                        for dc in range(8):
                            lhs = chB[:, dc * 512 + tt * 128: dc * 512 + tt * 128 + 128]
                            rhs = big2[:, 8192 + dc * 1024 + oc * 512:
                                       8192 + dc * 1024 + oc * 512 + 512]
                            P.op("t", (lambda lhs=lhs, rhs=rhs, bk=bk, dc=dc:
                                       lambda E: E.matmul(bank(bk), lhs, rhs,
                                                          start=(dc == 0), stop=False))(),
                                 after=aft if dc == 0 else ())
                        P.op("t", (lambda bk=bk, oc=oc: lambda E: E.matmul(
                            bank(bk), ORow, BRow[0:1, oc * 512:(oc + 1) * 512],
                            start=False, stop=True))(), inc=True)
                        mtok = P.tok("t")
                        P.op("s", (lambda tt=tt, oc=oc, bk=bk: lambda E: E.copy(
                            chC[:, tt * 1024 + oc * 512: tt * 1024 + oc * 512 + 512],
                            bank(bk)))(), after=(mtok, kv_last_mm), inc=True)
                        pr_ev[bk] = P.tok("s")
                vt_tok = P.tok("s")
                P.mark("s1_c%d_vproj" % tc)
                for tt in range(4):
                    for hh in range(16):
                        r0, c0 = 64 * (hh % 2), (hh // 2) * 64
                        lhs = chD[:, tt * 1024 + hh * 64: tt * 1024 + hh * 64 + 64]
                        rhs = chC[:, tt * 1024 + hh * 64: tt * 1024 + hh * 64 + 64]
                        o = psum[r0:r0 + 64, c0:c0 + 64]
                        P.op("t", (lambda o=o, lhs=lhs, rhs=rhs:
                                   lambda E: E.matmul(o, lhs, rhs, start=False,
                                                      stop=False, skip_group_check=True))(),
                             after=(vt_tok,) if (tt == 0 and hh == 0) else ())
                        o2 = psum[r0:r0 + 64, 512 + (hh // 2): 513 + (hh // 2)]
                        P.op("t", (lambda o2=o2, lhs=lhs:
                                   lambda E: E.matmul(o2, lhs, onescol[:], start=False,
                                                      stop=False, skip_group_check=True))(),
                             inc=(tt == 3 and hh == 15))
                kv_last_mm = P.tok("t")
                chunk_guard = kv_last_mm
                P.mark("s1_chunk%d" % tc)
            P.op("v", lambda E: E.memset(rowb[0:1, 0:512], 0.0),
                 after=(kv_last_mm,), inc=True)
            z2 = P.tok("v")
            P.op("t", lambda E: E.matmul(bank(0), ORow, rowb[0:1, 0:512],
                                         start=False, stop=True), after=(z2,))
            P.op("t", lambda E: E.matmul(bank(1, 0, 16), ORow, rowb[0:1, 0:16],
                                         start=False, stop=True), inc=True)
            kvcl = P.tok("t")
            P.op("s", lambda E: E.copy(kvsb[:], bank(0)), after=(kvcl,))
            P.op("s", lambda E: E.copy(ksumsb[:], bank(1, 0, 16)), inc=True)
            kvst = P.tok("s")
            P.dma("g", kvrs_in[0:128, 0:512], kvsb[:], after=(kvst,))
            P.dma("g", kvrs_in[128:256, 0:512], kvsb[:])
            P.dma("g", kvrs_in[0:128, 512:528], ksumsb[:])
            P.dma("g", kvrs_in[128:256, 512:528], ksumsb[:])
            spill_tok = P.tok("g")
            cc_kv = P.op("g", lambda E: E.collective_compute(
                "ReduceScatter", ALU.add, replica_groups=RG,
                ins=[kvrs_in], outs=[kvrs_out]),
                after=(spill_tok,), inc="c")
            kv_ld = P.dma("g", chC[:, 0:528], kvrs_out, after=(cc_kv,))
            P.op("v", lambda E: E.memset(kvblk[:], 0.0),
                 after=(kvcl, spill_tok), inc=True)
            zkb = P.tok("v")
            for j in range(8):
                P.op("s", (lambda j=j: lambda E: E.copy(
                    kvblk[0:64, j * 130: j * 130 + 64],
                    chC[0:64, j * 64:(j + 1) * 64]))(),
                    after=(zkb, kv_ld) if j == 0 else ())
                P.op("s", (lambda j=j: lambda E: E.copy(
                    kvblk[64:128, j * 130 + 65: j * 130 + 129],
                    chC[64:128, j * 64:(j + 1) * 64]))())
                P.op("s", (lambda j=j: lambda E: E.copy(
                    kvblk[0:64, j * 130 + 64: j * 130 + 65],
                    chC[0:64, 512 + j: 513 + j]))())
                P.op("s", (lambda j=j: lambda E: E.copy(
                    kvblk[64:128, j * 130 + 129: j * 130 + 130],
                    chC[64:128, 512 + j: 513 + j]))(), inc=(j == 7))
            P.barrier()
            P.mark("s1_kvclose")
            tap_sbuf("kv", lambda: kvsb[:])
            tap_sbuf("ksum", lambda: ksumsb[:])

            bro = P.dma("q", BRow, brow_o0)
            wq_t = P.dma("q", big2[:, 0:8192].rearrange("p (c o) -> p c o", o=1024),
                         wq0T.rearrange("(c p) o -> p c o", p=128))
            wo_t = P.dma("q", big2[:, 8192:16384].rearrange("p (c o) -> p c o", o=1024),
                         wo0T.rearrange("(c p) o -> p c o", p=128))
            pr_ev = {2: None, 3: None, 4: None, 5: None}
            chunk_guard = None
            for oc4 in range(4):
                xt = P.dma("q", chA[:].rearrange("p (n d) -> p n d", d=1024),
                           xown.rearrange("(n p) d -> p n d", p=128)[:, oc4 * 4:(oc4 + 1) * 4, :],
                           after=(chunk_guard,))
                lnt = emit_ln(chA, 4, xt)
                trt = emit_tr_chunk(chA, chB, lnt, busytok=chunk_guard)
                for ot in range(8):
                    bk = 2 + ot % 2
                    aft = (trt, wq_t, pr_ev[bk], chunk_guard)
                    for dc in range(8):
                        lhs = big2[:, dc * 1024 + ot * 128: dc * 1024 + ot * 128 + 128]
                        rhs = chB[:, dc * 512:(dc + 1) * 512]
                        P.op("t", (lambda lhs=lhs, rhs=rhs, bk=bk, dc=dc: lambda E: E.matmul(
                            bank(bk), lhs, rhs, start=(dc == 0), stop=(dc == 7)))(),
                            after=aft if dc == 0 else (), inc=(dc == 7))
                    mtok = P.tok("t")
                    P.op("v", (lambda ot=ot, bk=bk: lambda E: E.tensor_scalar(
                        chC[:, ot * 512:(ot + 1) * 512], bank(bk), bc0[:, ot:ot + 1],
                        None, ALU.add))(), after=(mtok,), inc=True)
                    pr_ev[bk] = P.tok("v")
                q_tok = P.tok("v")
                for pr in range(8):
                    bk = 2 + pr % 2
                    P.op("t", (lambda pr=pr, bk=bk: lambda E: E.matmul(
                        bank(bk), pblks[:], chC[:, pr * 512:(pr + 1) * 512],
                        start=True, stop=True))(),
                        after=(q_tok, pr_ev[bk], chunk_guard), inc=True)
                    mtok = P.tok("t")
                    P.op("v", (lambda pr=pr, bk=bk: lambda E: E.tensor_scalar(
                        chD[:, pr * 512:(pr + 1) * 512], bank(bk), 0.0, 1e-6,
                        ALU.max, ALU.add))(), after=(mtok,), inc=True)
                    pr_ev[bk] = P.tok("v")
                qf_tok = P.tok("v")
                for tt in range(4):
                    for j in range(8):
                        bk = 4 + j % 2
                        lhs = chD[:, j * 512 + tt * 128: j * 512 + tt * 128 + 128]
                        P.op("t", (lambda lhs=lhs, bk=bk, j=j: lambda E: E.matmul(
                            bank(bk, 0, 130), lhs,
                            kvblk[:, j * 130:(j + 1) * 130],
                            start=True, stop=True, skip_group_check=True))(),
                            after=(qf_tok, pr_ev[bk]), inc=True)
                        mtok = P.tok("t")
                        P.op("v", (lambda bk=bk: lambda E: E.tensor_scalar(
                            misc[:, 32:33], bank(bk, 64, 65), 1e-6, None, ALU.add))(),
                            after=(mtok,))
                        P.op("v", (lambda bk=bk: lambda E: E.tensor_scalar(
                            misc[:, 33:34], bank(bk, 129, 130), 1e-6, None, ALU.add))())
                        P.drain("v")
                        P.op("v", lambda E: E.reciprocal(misc[:, 34:36], misc[:, 32:34]))
                        P.drain("v")
                        P.op("v", (lambda tt=tt, j=j, bk=bk: lambda E: E.tensor_scalar(
                            chA[:, tt * 1024 + j * 128: tt * 1024 + j * 128 + 64],
                            bank(bk, 0, 64), misc[:, 34:35], None, ALU.mult))())
                        P.op("v", (lambda tt=tt, j=j, bk=bk: lambda E: E.tensor_scalar(
                            chA[:, tt * 1024 + j * 128 + 64: tt * 1024 + j * 128 + 128],
                            bank(bk, 65, 129), misc[:, 35:36], None, ALU.mult))(),
                            inc=True)
                        P.drain("v")
                        pr_ev[bk] = P.tok("v")
                at_tok = P.tok("v")
                trt2 = emit_tr_chunk(chA, chB, at_tok)
                xrt = P.dma("q", chA[:].rearrange("p (n d) -> p n d", d=1024),
                            xown.rearrange("(n p) d -> p n d", p=128)[:, oc4 * 4:(oc4 + 1) * 4, :],
                            after=(trt2,))
                for tt in range(4):
                    for oc in range(2):
                        bk = 2 + oc
                        aft = (trt2, wo_t, bro, pr_ev[bk])
                        for dc in range(8):
                            lhs = chB[:, dc * 512 + tt * 128: dc * 512 + tt * 128 + 128]
                            rhs = big2[:, 8192 + dc * 1024 + oc * 512:
                                       8192 + dc * 1024 + oc * 512 + 512]
                            P.op("t", (lambda lhs=lhs, rhs=rhs, bk=bk, dc=dc:
                                       lambda E: E.matmul(bank(bk), lhs, rhs,
                                                          start=(dc == 0), stop=False))(),
                                 after=aft if dc == 0 else ())
                        P.op("t", (lambda bk=bk, oc=oc: lambda E: E.matmul(
                            bank(bk), ORow, BRow[0:1, oc * 512:(oc + 1) * 512],
                            start=False, stop=True))(), inc=True)
                        mtok = P.tok("t")
                        sl = big1[:, (oc4 * 4 + tt) * 1024 + oc * 512:
                                  (oc4 * 4 + tt) * 1024 + oc * 512 + 512]
                        P.op("v", (lambda sl=sl, tt=tt, oc=oc, bk=bk:
                                   lambda E: E.scalar_tensor_tensor(
                                       sl, bank(bk), 0.0,
                                       chA[:, tt * 1024 + oc * 512: tt * 1024 + oc * 512 + 512],
                                       ALU.bypass, ALU.add))(), after=(mtok, xrt), inc=True)
                        pr_ev[bk] = P.tok("v")
                chunk_guard = P.tok("v")
                P.mark("s1_q%d" % oc4)
            P.barrier()
            P.mark("s1_end")
            tap_sbuf("x1", lambda: big1[:].rearrange("p (n d) -> p n d", d=1024), tr3=True)

        if stage_max >= 2:
            guard = None
            for tc in range(4):
                P.op("v", (lambda tc=tc: lambda E: E.tensor_copy(
                    chA[:], big1[:, tc * 4096:(tc + 1) * 4096]))(),
                    after=(guard,), inc=True)
                ct = P.tok("v")
                P.drain("v")
                lnt = emit_ln(chA, 4, ct)
                prev_evac = {0: None, 1: None}
                for dc in range(8):
                    bk = 6 + dc % 2
                    for tt in range(4):
                        in_ap = chA[:, tt * 1024 + dc * 128: tt * 1024 + dc * 128 + 128]
                        o = psum[:, bk * 512 + tt * 128: bk * 512 + tt * 128 + 128]
                        P.op("t", (lambda o=o, in_ap=in_ap: lambda E: E.transpose(
                            o, in_ap, ident[:]))(),
                            after=(lnt, prev_evac[dc % 2]) if tt == 0 else (),
                            inc=(tt == 3))
                    mtok = P.tok("t")
                    P.op("s", (lambda dc=dc, tc=tc, bk=bk: lambda E: E.copy(
                        big2[:, dc * 2048 + tc * 512: dc * 2048 + tc * 512 + 512],
                        bank(bk)))(), after=(mtok,), inc=True)
                    prev_evac[dc % 2] = P.tok("s")
                guard = P.tok("s")
            P.barrier()
            tap_sbuf("y1f", lambda: big2[:])

            prev_t = None
            prev_a = None
            for dc in range(8):
                ld = P.dma("q", chB[:, 0:4096], wfcT_in[dc * 128:(dc + 1) * 128, :],
                           after=(prev_a,))
                P.op("s", lambda E: E.activation(chC[:], chB[:, 0:4096], AF.Abs),
                     after=(ld, prev_t), inc=True)
                atok = P.tok("s")
                prev_a = atok
                for hs in range(8):
                    P.op("t", (lambda hs=hs, dc=dc: lambda E: E.matmul(
                        psum[0:1, hs * 512:(hs + 1) * 512], onescol[:],
                        chC[:, hs * 512:(hs + 1) * 512], start=(dc == 0), stop=(dc == 7)))(),
                        after=(atok,) if hs == 0 else (), inc=(hs == 7))
                prev_t = P.tok("t")
            sum_tok = prev_t
            chB_guard = prev_a
            for sub in (3, 2, 1, 0):
                P.op("s", (lambda sub=sub: lambda E: E.copy(
                    BRow, psum[0:1, sub * 1024:(sub + 1) * 1024]))(),
                    after=(sum_tok,), inc=True)
                et = P.tok("s")
                P.op("v", lambda E: E.tensor_scalar(BRow, BRow, 1.0 / D, 1e-5,
                                                    ALU.mult, ALU.max), after=(et,), inc=True)
                P.drain("v")
                rowt = P.tok("v")
                for j in range(8):
                    P.op("t", (lambda j=j: lambda E: E.transpose(
                        psum[0:128, 3584 + j:3585 + j], rowb[0:1, j * 128:(j + 1) * 128],
                        ident[0:1, 0:1]))(),
                        after=(rowt,) if j == 0 else (), inc=(j == 7))
                trt = P.tok("t")
                P.op("s", (lambda sub=sub: lambda E: E.copy(
                    s0c[:, sub * 8: sub * 8 + 8], psum[:, 3584:3592]))(),
                    after=(trt,), inc=True)
                s0t = P.tok("s")
                P.op("v", lambda E: E.reciprocal(BRow, BRow), after=(s0t,), inc=True)
                P.drain("v")
                rect = P.tok("v")
                for half in range(2):
                    P.op("t", (lambda half=half: lambda E: E.matmul(
                        psum[:, 3072 + half * 512: 3072 + (half + 1) * 512], ORow,
                        rowb[0:1, half * 512:(half + 1) * 512], start=True, stop=True))(),
                        after=(rect,) if half == 0 else (), inc=(half == 1))
                bt = P.tok("t")
                P.op("s", (lambda sub=sub: lambda E: E.copy(
                    chD[:, sub * 1024:(sub + 1) * 1024], psum[:, 3072:4096]))(),
                    after=(bt,), inc=True)
            rscb_tok = P.tok("s")
            ut = P.dma("q", chA[0:16, 0:4096], uT_in)
            P.op("v", lambda E: E.tensor_tensor(chA[0:16, :], chA[0:16, :],
                                                chD[0:16, :], ALU.mult),
                 after=(ut, rscb_tok), inc=True)
            utg_tok = P.tok("v")
            prev_sp = None
            for hb in range(8):
                ld = P.dma("q", chB[:, 0:4096].rearrange("p (c h) -> p c h", h=512),
                           wfcT_in.rearrange("(c p) h -> p c h", p=128)[
                               :, :, hb * 512:(hb + 1) * 512],
                           after=(prev_sp, chB_guard if hb == 0 else None))
                chB_guard = None
                qt = emit_quant(chB, chC, 4096,
                                bc_map=(lambda piece, hb=hb:
                                        chD[:, hb * 512:(hb + 1) * 512]),
                                lo=-1.0, hi=1.0, after=(ld, utg_tok, prev_sp))
                prev_sp = P.dma("g", tfc_s.rearrange("(c p) h -> p c h", p=128)[
                    :, :, hb * 512:(hb + 1) * 512],
                    chC[:].rearrange("p (c h) -> p c h", h=512), after=(qt,))
                chB_guard = qt
            P.barrier()
            tap_dram("tfc", tfc_s)
            tap_sbuf("s0c", lambda: s0c[:, 0:32])
            tap_sbuf("rscb", lambda: chD[:])

            prev_t = None
            prev_a = None
            for hc in range(32):
                ld = P.dma("q", chB[:, 0:1024], wprT_in[hc * 128:(hc + 1) * 128, :],
                           after=(prev_a, chB_guard if hc == 0 else None))
                if hc == 0:
                    chB_guard = None
                P.op("s", lambda E: E.activation(chC[:, 0:1024], chB[:, 0:1024], AF.Abs),
                     after=(ld, prev_t), inc=True)
                atok = P.tok("s")
                prev_a = atok
                for hs in range(2):
                    P.op("t", (lambda hs=hs, hc=hc: lambda E: E.matmul(
                        psum[0:1, hs * 512:(hs + 1) * 512], onescol[:],
                        chC[:, hs * 512:(hs + 1) * 512], start=(hc == 0), stop=(hc == 31)))(),
                        after=(atok,) if hs == 0 else (), inc=(hs == 1))
                prev_t = P.tok("t")
            chB_guard = prev_a
            P.op("s", lambda E: E.copy(BRow, psum[0:1, 0:1024]), after=(prev_t,), inc=True)
            et = P.tok("s")
            P.op("v", lambda E: E.tensor_scalar(BRow, BRow, 1.0 / HID, 1e-5,
                                                ALU.mult, ALU.max), after=(et,), inc=True)
            P.drain("v")
            rowt = P.tok("v")
            for half in range(2):
                P.op("t", (lambda half=half: lambda E: E.matmul(
                    psum[:, 3072 + half * 512: 3072 + (half + 1) * 512], ORow,
                    rowb[0:1, half * 512:(half + 1) * 512], start=True, stop=True))(),
                    after=(rowt,) if half == 0 else (), inc=(half == 1))
            bt = P.tok("t")
            P.op("s", lambda E: E.copy(sm1[:], psum[:, 3072:4096]), after=(bt,), inc=True)
            s2bt = P.tok("s")
            P.op("v", lambda E: E.reciprocal(BRow, BRow), after=(s2bt,), inc=True)
            P.drain("v")
            rect = P.tok("v")
            for half in range(2):
                P.op("t", (lambda half=half: lambda E: E.matmul(
                    psum[:, 3072 + half * 512: 3072 + (half + 1) * 512], ORow,
                    rowb[0:1, half * 512:(half + 1) * 512], start=True, stop=True))(),
                    after=(rect,) if half == 0 else (), inc=(half == 1))
            bt = P.tok("t")
            P.op("s", lambda E: E.copy(chD[:, 0:1024], psum[:, 3072:4096]),
                 after=(bt,), inc=True)
            rpj_tok = P.tok("s")
            prev_sp = None
            for hb in range(8):
                ld = P.dma("q", chB[:, 0:4096].rearrange("p (c o) -> p c o", o=1024),
                           wprT_in.rearrange("(c p) o -> p c o", p=128)[
                               :, hb * 4:(hb + 1) * 4, :],
                           after=(prev_sp, chB_guard if hb == 0 else None))
                chB_guard = None
                qt = emit_quant(chB, chC, 4096,
                                bc_map=(lambda piece:
                                        chD[:, (piece % 2) * 512:
                                            (piece % 2) * 512 + 512]),
                                lo=-1.0, hi=1.0, after=(ld, rpj_tok, prev_sp))
                prev_sp = P.dma("g", tpr_s.rearrange("(c p) o -> p c o", p=128)[
                    :, hb * 4:(hb + 1) * 4, :],
                    chC[:].rearrange("p (c o) -> p c o", o=1024), after=(qt,))
                chB_guard = qt
            P.barrier()
            tap_dram("tpr", tpr_s)

            prev_ev = None
            for tt in range(16):
                for dc in range(8):
                    lhs = big2[:, dc * 2048 + tt * 128: dc * 2048 + tt * 128 + 128]
                    P.op("t", (lambda lhs=lhs, dc=dc: lambda E: E.matmul(
                        psum[:, 0:16], lhs, vTs[:, dc * 16:(dc + 1) * 16],
                        start=(dc == 0), stop=(dc == 7)))(),
                        after=(prev_ev,) if dc == 0 else (), inc=(dc == 7))
                mtok = P.tok("t")
                P.op("s", lambda E: E.copy(misc[:, 0:16], psum[:, 0:16]),
                     after=(mtok,), inc=True)
                et = P.tok("s")
                P.op("t", lambda E: E.transpose(psum[0:16, 512:640], misc[:, 0:16],
                                                ident[:]), after=(et,), inc=True)
                mt2 = P.tok("t")
                P.op("s", (lambda tt=tt: lambda E: E.copy(
                    chB[0:16, tt * 128:(tt + 1) * 128], psum[0:16, 512:640]))(),
                    after=(mt2, chB_guard if tt == 0 else None), inc=True)
                prev_ev = P.tok("s")
            s1_tok = prev_ev
            s2g_prev = None
            for tc in range(4):
                P.op("t", (lambda tc=tc: lambda E: E.matmul(
                    psum[0:16, 1024:1536], fbTs[:], chB[0:16, tc * 512:(tc + 1) * 512],
                    start=True, stop=True))(), after=(s1_tok, s2g_prev), inc=True)
                mtok = P.tok("t")
                P.op("v", (lambda tc=tc: lambda E: E.tensor_scalar(
                    chB[0:16, 2048 + tc * 512: 2048 + (tc + 1) * 512],
                    psum[0:16, 1024:1536], gatebs[0:16, 0:1], None, ALU.mult))(),
                    after=(mtok,), inc=True)
                s2g_prev = P.tok("v")
            s2g_tok = P.tok("v")
            P.barrier()
            tap_sbuf("s2g", lambda: chB[0:16, 2048:4096])

            fc_ev = {0: None, 1: None}
            pj_ev = {2: None, 3: None}
            h_guard = None
            tern_guard = None
            for hb in range(8):
                tld = P.dma("q", chC[:].rearrange("p (c h) -> p c h", h=512),
                            tfc_s.rearrange("(c p) h -> p c h", p=128)[
                                :, :, hb * 512:(hb + 1) * 512],
                            after=(tern_guard,))
                tld2 = P.dma("q", chD[:].rearrange("p (c o) -> p c o", o=1024),
                             tpr_s.rearrange("(c p) o -> p c o", p=128)[
                                 :, hb * 4:(hb + 1) * 4, :],
                             after=(tern_guard,))
                for tc in range(4):
                    for hs in range(4):
                        bk = hs % 2
                        aft = (tld, fc_ev[bk], s2g_tok, h_guard)
                        for dc in range(8):
                            lhs = chC[:, dc * 512 + hs * 128: dc * 512 + hs * 128 + 128]
                            rhs = big2[:, dc * 2048 + tc * 512: dc * 2048 + tc * 512 + 512]
                            P.op("t", (lambda lhs=lhs, rhs=rhs, bk=bk, dc=dc:
                                       lambda E: E.matmul(bank(bk), lhs, rhs,
                                                          start=(dc == 0), stop=False))(),
                                 after=aft if dc == 0 else ())
                        lhsu = chA[0:16, hb * 512 + hs * 128: hb * 512 + hs * 128 + 128]
                        P.op("t", (lambda lhsu=lhsu, bk=bk, tc=tc: lambda E: E.matmul(
                            bank(bk), lhsu, chB[0:16, 2048 + tc * 512: 2048 + (tc + 1) * 512],
                            start=False, stop=True))(), inc=True)
                        mtok = P.tok("t")
                        scol = s0c[:, hb * 4 + hs: hb * 4 + hs + 1]
                        P.op("v", (lambda bk=bk, scol=scol: lambda E: E.tensor_scalar(
                            kvsb[:], bank(bk), scol, 0.0, ALU.mult, ALU.max))(),
                            after=(mtok,))
                        P.drain("v")
                        P.op("v", (lambda hs=hs: lambda E: E.tensor_tensor(
                            chB[:, hs * 512:(hs + 1) * 512], kvsb[:],
                            kvsb[:], ALU.mult))(), inc=True)
                        P.drain("v")
                        fc_ev[bk] = P.tok("v")
                    h_tok = P.tok("v")
                    for tt in range(4):
                        for oc in range(2):
                            bk = 2 + oc
                            aft = (h_tok, tld2, pj_ev[bk])
                            for hs in range(4):
                                lhs = chB[:, hs * 512 + tt * 128: hs * 512 + tt * 128 + 128]
                                rhs = chD[:, hs * 1024 + oc * 512: hs * 1024 + oc * 512 + 512]
                                P.op("t", (lambda lhs=lhs, rhs=rhs, bk=bk, hs=hs:
                                           lambda E: E.matmul(bank(bk), lhs, rhs,
                                                              start=(hs == 0),
                                                              stop=(hs == 3)))(),
                                     after=aft if hs == 0 else (), inc=(hs == 3))
                            mtok = P.tok("t")
                            P.op("v", (lambda bk=bk, oc=oc: lambda E: E.tensor_tensor(
                                kvsb[:], bank(bk),
                                sm1[:, oc * 512:(oc + 1) * 512], ALU.mult))(),
                                after=(mtok,))
                            P.drain("v")
                            sl = big1[:, (tc * 4 + tt) * 1024 + oc * 512:
                                      (tc * 4 + tt) * 1024 + oc * 512 + 512]
                            P.op("v", (lambda sl=sl: lambda E: E.tensor_tensor(
                                sl, sl, kvsb[:], ALU.add))(), inc=True)
                            P.drain("v")
                            pj_ev[bk] = P.tok("v")
                    h_guard = P.tok("v")
                tern_guard = P.tok("v")
            P.barrier()
            tap_sbuf("x2", lambda: big1[:].rearrange("p (n d) -> p n d", d=1024), tr3=True)

        if stage_max >= 3:
            t1 = P.dma("g", ccin1.rearrange("(n p) d -> p n d", p=128),
                       big1[:].rearrange("p (n d) -> p n d", d=1024))
            t2 = P.dma("g", xos.rearrange("(n p) d -> p n d", p=128),
                       big1[:].rearrange("p (n d) -> p n d", d=1024))
            cc1 = P.op("g", lambda E: E.collective_compute(
                "AllGather", ALU.bypass, replica_groups=RG, ins=[ccin1], outs=[ccout1]),
                after=(t1, t2, P.tok("g")), inc="c")
            P.barrier()
            tap_dram("x2full", ccout1)

            wk_t = P.dma("q", big2[:, 0:4096].rearrange("p (c o) -> p c o", o=512),
                         wk1T.rearrange("(c p) o -> p c o", p=128))
            wv_t = P.dma("q", big2[:, 4096:8192].rearrange("p (c o) -> p c o", o=512),
                         wv1T.rearrange("(c p) o -> p c o", p=128))
            wq_t = P.dma("q", big2[:, 8192:12288].rearrange("p (c o) -> p c o", o=512),
                         wq1T.rearrange("(c p) o -> p c o", p=128))
            wo_t = P.dma("q", big2[:, 12288:16384].rearrange("p (c o) -> p c o", o=1024),
                         wo1To.rearrange("(c p) o -> p c o", p=128))
            brv = P.dma("q", BRow[0:1, 0:512], brow_v1)
            for slot in range(4):
                P.op("v", (lambda slot=slot: lambda E: E.memset(
                    chC[:, slot * 520:(slot + 1) * 520].rearrange(
                        "p (h c) -> p h c", c=65)[:, :, 64:65], 1.0))(),
                    inc=(slot == 3))
            ones_tok = P.tok("v")
            pr_ev = {2: None, 3: None}
            chunk_guard = None
            prev_sp = None
            for tc in range(8):
                xt = P.dma("q", chA[:].rearrange("p (n d) -> p n d", d=1024),
                           ccout1.rearrange("(n p) d -> p n d", p=128)[:, tc * 4:(tc + 1) * 4, :],
                           after=(chunk_guard, cc1))
                lnt = emit_ln(chA, 4, xt)
                trt = emit_tr_chunk(chA, chB[:, 0:4096], lnt, busytok=chunk_guard)
                for ot in range(4):
                    bk = 2 + ot % 2
                    aft = (trt, wk_t, pr_ev[bk])
                    for dc in range(8):
                        lhs = big2[:, dc * 512 + ot * 128: dc * 512 + ot * 128 + 128]
                        rhs = chB[:, dc * 512:(dc + 1) * 512]
                        P.op("t", (lambda lhs=lhs, rhs=rhs, bk=bk, dc=dc: lambda E: E.matmul(
                            bank(bk), lhs, rhs, start=(dc == 0), stop=(dc == 7)))(),
                            after=aft if dc == 0 else (), inc=(dc == 7))
                    mtok = P.tok("t")
                    P.op("v", (lambda ot=ot, tc=tc, bk=bk: lambda E: E.tensor_scalar(
                        big1[:, ot * 4096 + tc * 512: ot * 4096 + tc * 512 + 512],
                        bank(bk), bc1[:, 4 + ot:5 + ot], None, ALU.add))(),
                        after=(mtok,), inc=True)
                    pr_ev[bk] = P.tok("v")
                for tt in range(4):
                    bk = 2 + tt % 2
                    aft = (wv_t, brv, pr_ev[bk])
                    for dc in range(8):
                        lhs = chB[:, dc * 512 + tt * 128: dc * 512 + tt * 128 + 128]
                        rhs = big2[:, 4096 + dc * 512: 4096 + dc * 512 + 512]
                        P.op("t", (lambda lhs=lhs, rhs=rhs, bk=bk, dc=dc: lambda E: E.matmul(
                            bank(bk), lhs, rhs, start=(dc == 0), stop=False))(),
                            after=aft if dc == 0 else ())
                    P.op("t", (lambda bk=bk: lambda E: E.matmul(
                        bank(bk), ORow, BRow[0:1, 0:512], start=False, stop=True))(),
                        inc=True)
                    mtok = P.tok("t")
                    P.op("s", (lambda tt=tt, bk=bk: lambda E: E.copy(
                        chC[:, tt * 520:(tt + 1) * 520].rearrange(
                            "p (h c) -> p h c", c=65)[:, :, 0:64],
                        bank(bk).rearrange("p (h c) -> p h c", c=64)))(),
                        after=(mtok, prev_sp, ones_tok), inc=True)
                    pr_ev[bk] = P.tok("s")
                    prev_sp = P.dma(
                        "g", vps[(tc * 4 + tt) * 128:(tc * 4 + tt + 1) * 128, :],
                        chC[:, tt * 520:(tt + 1) * 520], after=(pr_ev[bk],))
                for ot in range(4):
                    bk = 2 + ot % 2
                    aft = (wq_t, pr_ev[bk])
                    for dc in range(8):
                        lhs = big2[:, 8192 + dc * 512 + ot * 128:
                                   8192 + dc * 512 + ot * 128 + 128]
                        rhs = chB[:, dc * 512:(dc + 1) * 512]
                        P.op("t", (lambda lhs=lhs, rhs=rhs, bk=bk, dc=dc: lambda E: E.matmul(
                            bank(bk), lhs, rhs, start=(dc == 0), stop=(dc == 7)))(),
                            after=aft if dc == 0 else (), inc=(dc == 7))
                    mtok = P.tok("t")
                    P.op("v", (lambda ot=ot, bk=bk: lambda E: E.tensor_scalar(
                        chD[:, ot * 512:(ot + 1) * 512], bank(bk),
                        bc1[:, ot:ot + 1], None, ALU.add))(),
                        after=(mtok, prev_sp), inc=True)
                    pr_ev[bk] = P.tok("v")
                    prev_sp = P.dma("g", qs3[ot, :, tc * 512:(tc + 1) * 512],
                                    chD[:, ot * 512:(ot + 1) * 512],
                                    after=(pr_ev[bk],))
                chunk_guard = P.tok("t")
            P.barrier()
            tap_dram("q1", qs3)
            tap_dram("v1", vps)
            tap_sbuf("k1", lambda: big1[:])

            bt_ld = P.dma("q", chC[:, 0:3072], btiles_in[:, 0:3072])
            P.op("v", lambda E: E.tensor_scalar_mul(btb[:], chC[:, 0:3072], 8.0),
                 after=(bt_ld,), inc=True)
            P.barrier()

            exp_guard = {}
            av_guard = {}
            avadd_guard = {}
            prev_fin = None
            for qc in range(8):
                qld = P.dma("q", chA[:, 0:2048].rearrange("p (r c) -> p r c", c=512),
                            qs3[:, :, qc * 512:(qc + 1) * 512].rearrange("r p c -> p r c"),
                            after=(prev_fin,))
                vld_guard = None
                for ktg in range(4):
                    vld = []
                    for g in range(8):
                        kt = ktg * 8 + g
                        vld.append(P.dma("q", chB[:, g * 520:(g + 1) * 520],
                                         vps[kt * 128:(kt + 1) * 128, :],
                                         after=(vld_guard, prev_fin) if g == 0 else ()))
                    for pr in range(4):
                        hA, hB = 2 * pr, 2 * pr + 1
                        last_av = {}
                        for g in range(8):
                            kt = ktg * 8 + g
                            par = kt % 2
                            for hh in (hA, hB):
                                bk = par * 2 + (hh % 2)
                                r0 = 64 * (hh % 2)
                                dts = [qc * 4 + j - kt for j in range(4)]
                                bands = [j for j in range(4) if abs(dts[j]) <= 1]
                                lhs = big1[r0:r0 + 64, pr * 4096 + kt * 128:
                                           pr * 4096 + kt * 128 + 128]
                                rhs = chA[r0:r0 + 64, pr * 512:(pr + 1) * 512]
                                P.op("t", (lambda lhs=lhs, rhs=rhs, bk=bk,
                                           nb=len(bands): lambda E: E.matmul(
                                               bank(bk), lhs, rhs, start=True,
                                               stop=(nb == 0), skip_group_check=True))(),
                                     after=(qld, exp_guard.get(bk)),
                                     inc=(len(bands) == 0))
                                for i, j in enumerate(bands):
                                    dt = dts[j]
                                    rb = btb[:, (hh * 3 + dt + 1) * 128:
                                             (hh * 3 + dt + 2) * 128]
                                    o = psum[:, bk * 512 + j * 128: bk * 512 + j * 128 + 128]
                                    P.op("t", (lambda o=o, rb=rb,
                                               last=(i == len(bands) - 1):
                                               lambda E: E.matmul(
                                                   o, identb[:], rb, start=False,
                                                   stop=last, skip_group_check=True))(),
                                         inc=(i == len(bands) - 1))
                                smm = P.tok("t")
                                spans = []
                                jj = 0
                                while jj < 4:
                                    cls = ("band" if abs(dts[jj]) <= 1 else
                                           ("fp" if dts[jj] >= 2 else "fm"))
                                    j2 = jj
                                    while j2 + 1 < 4 and (
                                        ("band" if abs(dts[j2 + 1]) <= 1 else
                                         ("fp" if dts[j2 + 1] >= 2 else "fm")) == cls):
                                        j2 += 1
                                    spans.append((jj, j2, cls))
                                    jj = j2 + 1
                                pslot = 2048 + par * 1024 + (hh % 2) * 512
                                for i, (j0, j1, cls) in enumerate(spans):
                                    o = chA[:, pslot + j0 * 128: pslot + (j1 + 1) * 128]
                                    iap = psum[:, bk * 512 + j0 * 128:
                                               bk * 512 + (j1 + 1) * 128]
                                    if cls == "band":
                                        bias = 0.0
                                    elif cls == "fp":
                                        bias = rbf[:, 2 * hh + 1: 2 * hh + 2]
                                    else:
                                        bias = rbf[:, 2 * hh: 2 * hh + 1]
                                    P.op("s", (lambda o=o, iap=iap, bias=bias:
                                               lambda E: E.activation(
                                                   o, iap, AF.Exp, bias=bias,
                                                   scale=0.125))(),
                                        after=(smm, av_guard.get(bk)) if i == 0 else (),
                                        inc=(i == len(spans) - 1))
                                exp_guard[bk] = P.tok("s")
                                avbk = 4 + (hh % 2)
                                lhsv = chB[:, g * 520 + hh * 65: g * 520 + hh * 65 + 65]
                                rhsp = chA[:, pslot: pslot + 512]
                                aft_av = (exp_guard[bk], vld[g],
                                          avadd_guard.get(avbk) if g == 0 else None)
                                P.op("t", (lambda lhsv=lhsv, rhsp=rhsp, avbk=avbk, g=g:
                                           lambda E: E.matmul(
                                               bankp(avbk, 0, 65), lhsv, rhsp,
                                               start=(g == 0), stop=(g == 7),
                                               skip_group_check=True))(),
                                     after=aft_av, inc=True)
                                av_guard[bk] = P.tok("t")
                                last_av[hh] = av_guard[bk]
                        if pr == 0:
                            P.drain("v")
                        for hh in (hA, hB):
                            avbk = 4 + (hh % 2)
                            sl = chC[0:65, hh * 512:(hh + 1) * 512]
                            if ktg == 0:
                                P.op("v", (lambda sl=sl, avbk=avbk: lambda E: E.tensor_copy(
                                    sl, bankp(avbk, 0, 65)))(),
                                    after=(last_av[hh],), inc=True)
                            else:
                                P.op("v", (lambda sl=sl, avbk=avbk:
                                           lambda E: E.scalar_tensor_tensor(
                                               sl, bankp(avbk, 0, 65), 0.0, sl,
                                               ALU.bypass, ALU.add))(),
                                    after=(last_av[hh],), inc=True)
                            avadd_guard[avbk] = P.tok("v")
                    vld_guard = P.tok("v")
                P.drain("v")
                av_done = P.tok("v")
                tr_ev = {6: None, 7: None}
                for hh in range(8):
                    for j in range(4):
                        bk = 6 + (hh * 4 + j) % 2
                        iap = chC[0:65, hh * 512 + j * 128: hh * 512 + j * 128 + 128]
                        P.op("t", (lambda iap=iap, bk=bk: lambda E: E.transpose(
                            bank(bk, 0, 65), iap, ident[0:65, 0:65]))(),
                            after=(av_done, tr_ev[bk]), inc=True)
                        mtok = P.tok("t")
                        P.op("v", (lambda bk=bk: lambda E: E.reciprocal(
                            misc[:, 34:35], psum[:, bk * 512 + 64: bk * 512 + 65]))(),
                            after=(mtok,))
                        P.drain("v")
                        P.op("v", (lambda bk=bk, j=j, hh=hh: lambda E: E.tensor_scalar(
                            atm[:, j * 512 + hh * 64: j * 512 + hh * 64 + 64],
                            bank(bk, 0, 64), misc[:, 34:35], None, ALU.mult))(),
                            inc=True)
                        P.drain("v")
                        tr_ev[bk] = P.tok("v")
                atm_tok = P.tok("v")
                tr2_ev = {6: None, 7: None}
                for dc in range(4):
                    bk = 6 + dc % 2
                    for j in range(4):
                        iap = atm[:, j * 512 + dc * 128: j * 512 + dc * 128 + 128]
                        o = psum[:, bk * 512 + j * 128: bk * 512 + j * 128 + 128]
                        P.op("t", (lambda o=o, iap=iap: lambda E: E.transpose(
                            o, iap, ident[:]))(),
                            after=(atm_tok, tr2_ev[bk]) if j == 0 else (), inc=(j == 3))
                    mtok = P.tok("t")
                    P.op("s", (lambda dc=dc, bk=bk: lambda E: E.copy(
                        chC[:, dc * 512:(dc + 1) * 512], bank(bk)))(),
                        after=(mtok,), inc=True)
                    tr2_ev[bk] = P.tok("s")
                fm_tok = P.tok("s")
                woe_guard = {0: None, 1: None}
                for j in range(4):
                    for oc in range(2):
                        bk = oc
                        aft = (fm_tok, wo_t, woe_guard[bk])
                        for dc in range(4):
                            lhs = chC[:, dc * 512 + j * 128: dc * 512 + j * 128 + 128]
                            rhs = big2[:, 12288 + dc * 1024 + oc * 512:
                                       12288 + dc * 1024 + oc * 512 + 512]
                            P.op("t", (lambda lhs=lhs, rhs=rhs, bk=bk, dc=dc:
                                       lambda E: E.matmul(bank(bk), lhs, rhs,
                                                          start=(dc == 0),
                                                          stop=(dc == 3)))(),
                                 after=aft if dc == 0 else (), inc=(dc == 3))
                        mtok = P.tok("t")
                        P.op("s", (lambda bk=bk, oc=oc: lambda E: E.copy(
                            chB[:, oc * 512:(oc + 1) * 512], bank(bk)))(),
                            after=(mtok,), inc=True)
                        etok = P.tok("s")
                        woe_guard[bk] = P.dma(
                            "g", ccin2[qc * 512 + j * 128: qc * 512 + j * 128 + 128,
                                       oc * 512:(oc + 1) * 512],
                            chB[:, oc * 512:(oc + 1) * 512], after=(etok,))
                prev_fin = P.tok("g")
            P.barrier()
            cc2 = P.op("g", lambda E: E.collective_compute(
                "ReduceScatter", ALU.add, replica_groups=RG, ins=[ccin2], outs=[ccout2]),
                after=(P.tok("g"),), inc="c")
            P.barrier()
            tap_dram("wo1", ccout2)
            bobt = P.dma("q", chC[:, 0:1024], bob_in)
            ld_guard = None
            for tt in range(16):
                r1 = P.dma("q", chA[:, 0:1024], ccout2[tt * 128:(tt + 1) * 128, :],
                           after=(cc2, ld_guard))
                r2 = P.dma("q", chB[:, 0:1024], xos[tt * 128:(tt + 1) * 128, :],
                           after=(ld_guard,))
                P.op("v", lambda E: E.tensor_tensor(chA[:, 0:1024], chA[:, 0:1024],
                                                    chC[:, 0:1024], ALU.add),
                     after=(r1, bobt))
                P.drain("v")
                P.op("v", (lambda tt=tt: lambda E: E.tensor_tensor(
                    big1[:, tt * 1024:(tt + 1) * 1024], chA[:, 0:1024],
                    chB[:, 0:1024], ALU.add))(), after=(r2,), inc=True)
                P.drain("v")
                ld_guard = P.tok("v")
            P.barrier()
            tap_sbuf("x3", lambda: big1[:].rearrange("p (n d) -> p n d", d=1024), tr3=True)

        if stage_max >= 4:
            guard = None
            xq_tr_guard = None
            xq_ev = {6: None, 7: None}
            for tc in range(4):
                P.op("v", (lambda tc=tc: lambda E: E.tensor_copy(
                    chA[:], big1[:, tc * 4096:(tc + 1) * 4096]))(),
                    after=(guard,), inc=True)
                ct = P.tok("v")
                P.drain("v")
                lnt = emit_ln(chA, 4, ct)
                for tt4 in range(4):
                    tt = tc * 4 + tt4
                    xt = chA[:, tt4 * 1024:(tt4 + 1) * 1024]
                    P.op("v", (lambda xt=xt: lambda E: E.tensor_reduce(
                        misc[:, 40:41], xt, AX, ALU.max, apply_absolute_value=True))(),
                        after=(lnt,))
                    P.drain("v")
                    P.op("v", (lambda tt=tt: lambda E: E.tensor_scalar_max(
                        s0c[:, 32 + tt:33 + tt], misc[:, 40:41], 1e-5))())
                    P.drain("v")
                    P.op("v", (lambda tt=tt: lambda E: E.reciprocal(
                        misc[:, 41:42], s0c[:, 32 + tt:33 + tt]))())
                    P.drain("v")
                    P.op("v", lambda E: E.tensor_scalar_mul(misc[:, 42:43],
                                                            misc[:, 41:42], 127.0), inc=True)
                    P.drain("v")
                    sct = P.tok("v")
                    qt = emit_quant(xt, chB, 1024, scol=misc[:, 42:43],
                                    lo=-128.0, hi=127.0, after=(sct, xq_tr_guard))
                    for dc in range(8):
                        bk = 6 + dc % 2
                        P.op("t", (lambda dc=dc, bk=bk: lambda E: E.transpose(
                            bank(bk, 0, 128), chB[:, dc * 128:(dc + 1) * 128],
                            ident[:]))(), after=(qt, xq_ev[bk]), inc=True)
                        mtok = P.tok("t")
                        P.op("s", (lambda dc=dc, tt=tt, bk=bk: lambda E: E.copy(
                            xqfm[:, dc * 2048 + tt * 128: dc * 2048 + tt * 128 + 128],
                            bank(bk, 0, 128)))(), after=(mtok,), inc=True)
                        xq_ev[bk] = P.tok("s")
                    xq_tr_guard = P.tok("t")
                guard = P.tok("s")
            xq_tok = guard
            P.barrier()

            def global_scale(win, outcol, nchunks, ncols, guard=None):
                prev_t2 = None
                prev_a2 = guard
                for ci in range(nchunks):
                    ld = P.dma("q", chB[:, 0:ncols], win[ci * 128:(ci + 1) * 128, :],
                               after=(prev_a2,))
                    P.op("s", (lambda ncols=ncols: lambda E: E.activation(
                        chC[:, 0:ncols], chB[:, 0:ncols], AF.Abs))(),
                        after=(ld, prev_t2), inc=True)
                    atok = P.tok("s")
                    prev_a2 = atok
                    for hs in range(ncols // 512):
                        P.op("t", (lambda hs=hs, ci=ci, nchunks=nchunks:
                                   lambda E: E.matmul(
                                       psum[0:1, hs * 512:(hs + 1) * 512], onescol[:],
                                       chC[:, hs * 512:(hs + 1) * 512],
                                       start=(ci == 0), stop=(ci == nchunks - 1)))(),
                             after=(atok,) if hs == 0 else (),
                             inc=(hs == ncols // 512 - 1))
                    prev_t2 = P.tok("t")
                P.op("v", (lambda ncols=ncols: lambda E: E.tensor_reduce(
                    misc[0:1, 44:45], psum[0:1, 0:ncols], AX, ALU.add))(),
                    after=(prev_t2,))
                P.drain("v")
                P.op("v", lambda E: E.tensor_scalar(misc[0:1, 45:46], misc[0:1, 44:45],
                                                    1.0 / (D * HID), 1e-5,
                                                    ALU.mult, ALU.max), inc=True)
                P.drain("v")
                vt_ = P.tok("v")
                P.op("t", lambda E: E.matmul(psum[:, 3072:3073], ORow, misc[0:1, 45:46],
                                             start=True, stop=True),
                     after=(vt_,), inc=True)
                mt_ = P.tok("t")
                P.op("s", (lambda outcol=outcol: lambda E: E.copy(
                    s0c[:, outcol:outcol + 1], psum[:, 3072:3073]))(),
                    after=(mt_,), inc=True)
                et_ = P.tok("s")
                P.op("v", (lambda outcol=outcol: lambda E: E.reciprocal(
                    s0c[:, outcol + 1:outcol + 2], s0c[:, outcol:outcol + 1]))(),
                    after=(et_,), inc=True)
                P.drain("v")
                return P.tok("v"), prev_a2

            g_t, g_abs = global_scale(wgT_in, 48, 8, 4096)
            u_t, u_abs = global_scale(wuT_in, 50, 8, 4096, guard=g_abs)

            guev_guard = None
            gu_spill_tok = None
            for hb in range(8):
                tgld = P.dma("q", chC[:].rearrange("p (c h) -> p c h", h=512),
                             wgT_in.rearrange("(c p) h -> p c h", p=128)[
                                 :, :, hb * 512:(hb + 1) * 512],
                             after=(guev_guard, u_t if hb == 0 else None))
                tg = emit_quant(chC, terngb, 4096, scol=s0c[:, 49:50],
                                lo=-1.0, hi=1.0, after=(tgld, g_t))
                tuld = P.dma("q", chD[:].rearrange("p (c h) -> p c h", h=512),
                             wuT_in.rearrange("(c p) h -> p c h", p=128)[
                                 :, :, hb * 512:(hb + 1) * 512],
                             after=(guev_guard, u_t if hb == 0 else None))
                tu = emit_quant(chD, ternub, 4096, scol=s0c[:, 51:52],
                                lo=-1.0, hi=1.0, after=(tuld, u_t))
                for tt in range(16):
                    for src, bk0, tok0 in ((terngb, 0, tg), (ternub, 2, tu)):
                        bk = bk0 + tt % 2
                        aft = (tok0, xq_tok, guev_guard)
                        for dc in range(8):
                            lhs = xqfm[:, dc * 2048 + tt * 128: dc * 2048 + tt * 128 + 128]
                            rhs = src[:, dc * 512:(dc + 1) * 512]
                            P.op("t", (lambda lhs=lhs, rhs=rhs, bk=bk, dc=dc:
                                       lambda E: E.matmul(bank(bk), lhs, rhs,
                                                          start=(dc == 0),
                                                          stop=(dc == 7)))(),
                                 after=aft if dc == 0 else (), inc=(dc == 7))
                    gmt = P.tok("t")
                    bkg, bku = 0 + tt % 2, 2 + tt % 2
                    P.op("v", (lambda tt=tt: lambda E: E.tensor_scalar(
                        misc[:, 46:47], s0c[:, 32 + tt:33 + tt], 1.0 / 127.0,
                        None, ALU.mult))(), after=(gmt,))
                    P.drain("v")
                    P.op("v", lambda E: E.tensor_tensor(misc[:, 47:48], misc[:, 46:47],
                                                        s0c[:, 48:49], ALU.mult))
                    P.op("v", lambda E: E.tensor_tensor(misc[:, 48:49], misc[:, 46:47],
                                                        s0c[:, 50:51], ALU.mult), inc=True)
                    P.drain("v")
                    sct = P.tok("v")
                    P.op("s", (lambda bkg=bkg: lambda E: E.activation(
                        sm1[:, 0:512], bank(bkg), AF.Silu, scale=misc[:, 47:48]))(),
                        after=(sct,), inc=True)
                    silt = P.tok("s")
                    P.op("v", (lambda bku=bku: lambda E: E.scalar_tensor_tensor(
                        sm1[:, 512:1024], bank(bku), misc[:, 48:49], sm1[:, 0:512],
                        ALU.mult, ALU.mult))(), after=(silt, gu_spill_tok), inc=True)
                    gut = P.tok("v")
                    P.drain("v")
                    P.op("v", lambda E: E.tensor_reduce(misc[:, 49:50], sm1[:, 512:1024],
                                                        AX, ALU.max,
                                                        apply_absolute_value=True))
                    P.drain("v")
                    if hb == 0:
                        P.op("v", (lambda tt=tt: lambda E: E.tensor_copy(
                            stats[:, tt:tt + 1], misc[:, 49:50]))(), inc=True)
                    else:
                        P.op("v", (lambda tt=tt: lambda E: E.tensor_tensor(
                            stats[:, tt:tt + 1], stats[:, tt:tt + 1],
                            misc[:, 49:50], ALU.max))(), inc=True)
                    gmx = P.tok("v")
                    gu_spill_tok = P.dma(
                        "g", gus.rearrange("(n p) h -> p n h", p=128)[
                            :, tt, hb * 512:(hb + 1) * 512], sm1[:, 512:1024],
                        after=(gmx,))
                    guev_guard = P.tok("g")
            P.barrier()
            tap_dram("gu", gus)

            d_t, _d_abs = global_scale(wdT_in, 52, 32, 1024)
            tq_guard = None
            for hc in range(32):
                ld = P.dma("q", chC[:, 0:1024], wdT_in[hc * 128:(hc + 1) * 128, :],
                           after=(tq_guard, d_t if hc == 0 else None))
                tq_guard = emit_quant(chC[:, 0:1024],
                                      terndb[:, hc * 1024:(hc + 1) * 1024], 1024,
                                      scol=s0c[:, 53:54], lo=-1.0, hi=1.0,
                                      after=(ld, d_t))
            tern_d = tq_guard
            P.barrier()

            out_guard = None
            gld_guard = None
            tdq_guard = None
            for tt in range(16):
                gld = P.dma("q", chD[:], gus[tt * 128:(tt + 1) * 128, :],
                            after=(gld_guard,))
                P.op("v", (lambda tt=tt: lambda E: E.tensor_scalar_max(
                    misc[:, 50:51], stats[:, tt:tt + 1], 1e-5))(), after=(gld,))
                P.drain("v")
                P.op("v", lambda E: E.reciprocal(misc[:, 51:52], misc[:, 50:51]))
                P.drain("v")
                P.op("v", lambda E: E.tensor_scalar_mul(misc[:, 52:53],
                                                        misc[:, 51:52], 127.0))
                P.op("v", lambda E: E.tensor_scalar(misc[:, 53:54], misc[:, 50:51],
                                                    1.0 / 127.0, None, ALU.mult))
                P.drain("v")
                P.op("v", lambda E: E.tensor_tensor(misc[:, 54:55], misc[:, 53:54],
                                                    s0c[:, 52:53], ALU.mult), inc=True)
                P.drain("v")
                sct = P.tok("v")
                qt = emit_quant(chD, chC, 4096, scol=misc[:, 52:53],
                                lo=-128.0, hi=127.0, after=(sct, tdq_guard))
                ev2 = {6: None, 7: None}
                for hc in range(32):
                    bk = 6 + hc % 2
                    P.op("t", (lambda hc=hc, bk=bk: lambda E: E.transpose(
                        bank(bk, 0, 128), chC[:, hc * 128:(hc + 1) * 128],
                        ident[:]))(), after=(qt, ev2[bk]), inc=True)
                    mtok = P.tok("t")
                    P.op("s", (lambda hc=hc, bk=bk: lambda E: E.copy(
                        guqfm[:, hc * 128:(hc + 1) * 128], bank(bk, 0, 128)))(),
                        after=(mtok,), inc=True)
                    ev2[bk] = P.tok("s")
                gq_tok = P.tok("s")
                tdq_guard = P.tok("t")
                for oc in range(2):
                    bk = oc
                    aft = (gq_tok, tern_d, out_guard)
                    for hc in range(32):
                        lhs = guqfm[:, hc * 128:(hc + 1) * 128]
                        rhs = terndb[:, hc * 1024 + oc * 512: hc * 1024 + oc * 512 + 512]
                        P.op("t", (lambda lhs=lhs, rhs=rhs, bk=bk, hc=hc:
                                   lambda E: E.matmul(bank(bk), lhs, rhs,
                                                      start=(hc == 0),
                                                      stop=(hc == 31)))(),
                             after=aft if hc == 0 else (), inc=(hc == 31))
                    mtok = P.tok("t")
                    sl = big1[:, tt * 1024 + oc * 512: tt * 1024 + oc * 512 + 512]
                    P.op("v", (lambda sl=sl, bk=bk: lambda E: E.scalar_tensor_tensor(
                        sl, bank(bk), misc[:, 54:55], sl, ALU.mult, ALU.add))(),
                        after=(mtok,), inc=True)
                    out_guard = P.tok("v")
                    P.dma("g", yout[tt * 128:(tt + 1) * 128, oc * 512:(oc + 1) * 512],
                          sl, after=(out_guard,))
                gld_guard = P.tok("g")
            P.barrier()

        P.emit(sems, block)
    return nc


_NC_CACHE = {}


def _get_nc(taps, stage_max):
    key = (tuple(taps), stage_max)
    if key not in _NC_CACHE:
        _NC_CACHE[key] = build_program(taps=taps, stage_max=stage_max)
    return _NC_CACHE[key]


def kernel(**inputs):
    f32 = np.float32
    x = np.asarray(inputs["L0_attn_x"], f32)

    def T(a):
        return np.ascontiguousarray(np.asarray(a, f32).T)

    wq0T = T(inputs["L0_attn_Wq"]); wk0T = T(inputs["L0_attn_Wk"])
    wv0T = T(inputs["L0_attn_Wv"]); wo0T = T(inputs["L0_attn_Wo"])
    bq0 = np.asarray(inputs["L0_attn_bq"], f32)
    bk0 = np.asarray(inputs["L0_attn_bk"], f32)
    bv0 = np.asarray(inputs["L0_attn_bv"], f32)
    bo0 = np.asarray(inputs["L0_attn_bo"], f32)
    P0 = np.asarray(inputs["L0_attn_P"], f32)
    fast_b = np.asarray(inputs["L0_ffn_fast_b"], f32)
    wfcT = T(inputs["L0_ffn_w_fc"]); wprT = T(inputs["L0_ffn_w_proj"])
    uT = T(inputs["L0_ffn_u"]); vT = T(inputs["L0_ffn_v"])
    gate = np.asarray(inputs["L0_ffn_gate"], f32)
    wq1Tf = T(inputs["L1_attn_Wq"]); wk1Tf = T(inputs["L1_attn_Wk"])
    wv1Tf = T(inputs["L1_attn_Wv"]); wo1Tf = T(inputs["L1_attn_Wo"])
    bq1 = np.asarray(inputs["L1_attn_bq"], f32)
    bk1 = np.asarray(inputs["L1_attn_bk"], f32)
    bv1 = np.asarray(inputs["L1_attn_bv"], f32)
    bo1 = np.asarray(inputs["L1_attn_bo"], f32)
    relb = np.asarray(inputs["L1_attn_rel_bias"], f32)
    wgT = T(inputs["L1_ffn_w_gate"]); wuT = T(inputs["L1_ffn_w_up"])
    wdT = T(inputs["L1_ffn_w_down"])

    pblk = np.zeros((128, 128), f32)
    pblk[0:64, 0:64] = P0.T
    pblk[64:128, 64:128] = P0.T

    a = np.arange(128)
    bbx = np.arange(128)
    idx = {}
    for dt in (-1, 0, 1):
        rel = 128 * dt + a[None, :] - bbx[:, None]
        idx[dt] = np.clip(rel, -128, 128) + 128

    import ml_dtypes
    ident = np.eye(128, dtype=f32)
    identb = np.eye(128, dtype=ml_dtypes.bfloat16)
    ones_row = np.ones((1, 128), f32)

    taps = tuple(_TAPS)
    nc = _get_nc(taps, _STAGE_MAX)

    in_maps = []
    for c in range(8):
        b = c // 2
        hf = c % 2
        lo, hi = hf * 512, hf * 512 + 512
        heads = range(hf * 8, hf * 8 + 8)
        bt = np.empty((128, 8 * 3 * 128), f32)
        for i, h in enumerate(heads):
            for j, dt in enumerate((-1, 0, 1)):
                bt[:, (i * 3 + j) * 128:(i * 3 + j + 1) * 128] = relb[h][idx[dt]]
        rbfa = np.empty((128, 16), f32)
        for i, h in enumerate(heads):
            rbfa[:, 2 * i] = relb[h, 0]
            rbfa[:, 2 * i + 1] = relb[h, 256]
        im = {
            "xfull": np.ascontiguousarray(x[b]),
            "xown": np.ascontiguousarray(x[b, hf * 2048:(hf + 1) * 2048]),
            "wq0T": wq0T, "wk0T": wk0T, "wv0T": wv0T, "wo0T": wo0T,
            "bcols0": np.ascontiguousarray(np.concatenate(
                [bq0.reshape(8, 128).T, bk0.reshape(8, 128).T], axis=1)),
            "brow_v0": bv0.reshape(1, -1), "brow_o0": bo0.reshape(1, -1),
            "pblk": pblk, "fbT": np.ascontiguousarray(fast_b.T),
            "vT": vT, "uT": uT,
            "gateb": np.broadcast_to(gate.reshape(1, 1), (128, 1)).copy(),
            "wfcT": wfcT, "wprT": wprT,
            "wq1T": np.ascontiguousarray(wq1Tf[:, lo:hi]),
            "wk1T": np.ascontiguousarray(wk1Tf[:, lo:hi]),
            "wv1T": np.ascontiguousarray(wv1Tf[:, lo:hi]),
            "bcols1": np.ascontiguousarray(np.concatenate(
                [bq1[lo:hi].reshape(4, 128).T, bk1[lo:hi].reshape(4, 128).T], axis=1)),
            "brow_v1": bv1[lo:hi].reshape(1, -1),
            "wo1To": np.ascontiguousarray(wo1Tf[lo:hi, :]),
            "bob": np.broadcast_to(bo1.reshape(1, -1), (128, 1024)).copy(),
            "btiles": bt, "rbf": rbfa,
            "wgT": wgT, "wuT": wuT, "wdT": wdT,
            "ident": ident, "identb": identb, "ones_row": ones_row,
        }
        in_maps.append(im)

    res = run_bass_kernel_spmd(nc, in_maps, core_ids=list(range(8)))
    if taps:
        kernel._tap_results = [
            {k: v for k, v in r.items() if k.startswith("tap_")} for r in res.results]
    out = np.empty((4, 4096, 1024), np.float32)
    for c in range(8):
        b, hf = c // 2, c % 2
        out[b, hf * 2048:(hf + 1) * 2048] = res.results[c]["y"]
    return out
